# version 25
# baseline (speedup 1.0000x reference)
import os
import numpy as np
import concourse.bass as bass
import concourse.mybir as mybir
from concourse.bass_utils import run_bass_kernel_spmd

F32 = mybir.dt.float32
BF16 = mybir.dt.bfloat16
F16 = mybir.dt.float16
I32 = mybir.dt.int32
AF = mybir.ActivationFunctionType
OP = mybir.AluOpType
AX = mybir.AxisListType


def _build(D, P, L, N, R):
    assert R == 128 and D % 512 == 0 and P % 128 == 0 and P == L
    QL = 512 if L % 512 == 0 else L
    T = P + L
    nD = D // 128
    nJ = D // 128
    nTT = L // 128
    nTc = T // 128
    XP = R + 2 * N
    JG = 512
    nJG = D // JG
    JT_PER_G = JG // 128
    TTH = min(4, nTT)
    nHALF = nTT // TTH

    nc = bass.Bass("TRN2", target_bir_lowering=False)

    x_all = nc.dram_tensor("x_all", [T, D], F32, kind="ExternalInput").ap()
    A_log = nc.dram_tensor("A_log", [D, N], F32, kind="ExternalInput").ap()
    D_vec = nc.dram_tensor("D_vec", [D], F32, kind="ExternalInput").ap()
    xpw = nc.dram_tensor("xpw", [XP, D], F32, kind="ExternalInput").ap()
    dtw = nc.dram_tensor("dtw", [D, R], F32, kind="ExternalInput").ap()
    dtb = nc.dram_tensor("dtb", [D], F32, kind="ExternalInput").ap()
    W = nc.dram_tensor("W", [D, D], F32, kind="ExternalInput").ap()
    ident_in = nc.dram_tensor("ident", [128, 128], BF16, kind="ExternalInput").ap()
    out_hbm = nc.dram_tensor("out", [L, D], F32, kind="ExternalOutput").ap()

    wq_hbm = nc.dram_tensor("wq_st", [nD, 128, nJ * 128], BF16).ap()
    wsc_hbm = nc.dram_tensor("wsc_st", [D], F32).ap()
    xT_hbm = nc.dram_tensor("xT_st", [nD, 128, T], BF16).ap()
    dt_hbm = nc.dram_tensor("dt_st", [nD, 128, T], BF16).ap()
    u_hbm = nc.dram_tensor("u_st", [nD, 128, T], BF16).ap()
    gd_hbm = nc.dram_tensor("gd_st", [nD, 128, L], BF16).ap()
    bc_hbm = nc.dram_tensor("bc_st", [2 * N, T], BF16).ap()
    y_hbm = nc.dram_tensor("y_st", [nD, 128, L], BF16).ap()
    xm_hbm = nc.dram_tensor("xm_st", [nTT, 128, D], F32).ap()
    f16_hbm = nc.dram_tensor("f16_st", [nTT, 128, D], F16).ap()
    DBG = os.environ.get("K_DEBUG") == "1"
    if DBG:
        dbg_scum = nc.dram_tensor("dbg_scum", [128, T], F32, kind="ExternalOutput").ap()
        dbg_bias = nc.dram_tensor("dbg_bias", [128, 16], F32, kind="ExternalOutput").ap()
        dbg_dA = nc.dram_tensor("dbg_dA", [128, T], BF16, kind="ExternalOutput").ap()
        dbg_dbx = nc.dram_tensor("dbg_dbx", [128, T], BF16, kind="ExternalOutput").ap()
        dbg_br = nc.dram_tensor("dbg_br", [128, T], BF16, kind="ExternalOutput").ap()
        dbg_carry = nc.dram_tensor("dbg_carry", [128, 512], F32, kind="ExternalOutput").ap()

    SLAB = max(2 * N * P + 8 * P + 2 * P + 2 * P + 4 * L + 4 * P + L + P,
               3 * D + T,
               2 * T + T + 2 * T,
               512 + D + nD * 128 * TTH + nD * JG)
    SLAB = max(SLAB, 4 * D, max(5 * T + L, 3 * D) + nD * 128 + T)

    from contextlib import ExitStack
    with ExitStack() as _es:
        def _e(cm):
            return _es.enter_context(cm)
        slab = _e(nc.sbuf_tensor("slab", [128, SLAB], BF16))
        fa = _e(nc.sbuf_tensor("fa", [128, D], F32))
        fb = _e(nc.sbuf_tensor("fb", [128, D], F32))
        fc = _e(nc.sbuf_tensor("fc", [128, D], F32))
        fd = _e(nc.sbuf_tensor("fd", [128, D], F32))
        iq = _e(nc.sbuf_tensor("iq", [128, D // 2], I32))
        wsrep = _e(nc.sbuf_tensor("wsrep", [128, D], F32))
        A_sb = _e(nc.sbuf_tensor("A_sb", [128, nD * N], F32))
        dtb_sb = _e(nc.sbuf_tensor("dtb_sb", [128, nD], F32))
        Dv_sb = _e(nc.sbuf_tensor("Dv_sb", [128, nD], F32))

        ident = _e(nc.sbuf_tensor("ident_sb", [128, 128], BF16))
        
        
        carry = _e(nc.sbuf_tensor("carry", [128, nD * N], F32))
        st = _e(nc.sbuf_tensor("st", [128, 16], F32))
        asc = _e(nc.sbuf_tensor("asc", [128, nTT], F32))
        qsc = _e(nc.sbuf_tensor("qsc", [128, nTT], F32))
        bias16 = _e(nc.sbuf_tensor("bias16", [128, 16], F32))
        f16a = _e(nc.sbuf_tensor("f16a", [128, 512], F16))
        f16b = _e(nc.sbuf_tensor("f16b", [128, 512], F16))
        slab16 = _e(nc.sbuf_tensor("slab16", [128, 4 * 512], F16))
        ps_big = _e(nc.psum_tensor("ps_big", [128, 2048], F32))
        py0 = _e(nc.psum_tensor("py0", [128, 1024], F32))
        ps_tr = _e(nc.psum_tensor("ps_tr", [128, 2048], BF16))
        m_ld = _e(nc.semaphore("m_ld"))
        m_st = _e(nc.semaphore("m_st"))
        m_v = _e(nc.semaphore("m_v"))
        m_s = _e(nc.semaphore("m_s"))
        m_t = _e(nc.semaphore("m_t"))
        m_o = _e(nc.semaphore("m_o"))
        m_l0 = _e(nc.semaphore("m_l0"))
        m_l1 = _e(nc.semaphore("m_l1"))
        m_v2 = _e(nc.semaphore("m_v2"))
        m_s2 = _e(nc.semaphore("m_s2"))
        m_t2 = _e(nc.semaphore("m_t2"))
        m_st2 = _e(nc.semaphore("m_st2"))
        m_l2 = _e(nc.semaphore("m_l2"))
        m_p = _e(nc.semaphore("m_p"))
        blk = _e(nc.Block())
        o_dtwT = max(5 * T + L, 3 * D)
        dtwT = slab[:, o_dtwT : o_dtwT + nD * 128]
        o_bc = o_dtwT + nD * 128
        bc_sb = slab[:, o_bc : o_bc + T]
        o_xpwT = o_bc + T
        xpwT = slab[:, o_xpwT : o_xpwT + nD * XP]
        assert o_xpwT + nD * XP <= SLAB - 2 * D
        C = {"ld": 0, "st": 0, "v": 0, "s": 0, "t": 0, "o": 0, "l0": 0, "l1": 0,
             "v2": 0, "s2": 0, "t2": 0, "st2": 0, "l2": 0, "p": 0}
        SEM = {"ld": m_ld, "st": m_st, "v": m_v, "s": m_s, "t": m_t, "o": m_o,
               "l0": m_l0, "l1": m_l1, "v2": m_v2, "s2": m_s2, "t2": m_t2,
               "st2": m_st2, "l2": m_l2, "p": m_p}
        m_l = [m_l0, m_l1]

        def inc(instr, key, n=1):
            instr.then_inc(SEM[key], n)
            C[key] += n

        def vdone(v):
            v.drain()
            v.sem_inc(m_v, 1)
            C["v"] += 1

        def sdone(s):
            s.drain()
            s.sem_inc(m_s, 1)
            C["s"] += 1

        def vfin(instr):
            instr.then_inc(m_v, 1)
            C["v"] += 1

        def sfin(instr):
            instr.then_inc(m_s, 1)
            C["s"] += 1

        @blk.sync
        def _(sy):
            src = bass.AP(A_log.tensor, 0, [[N, 128], [128 * N, nD], [1, N]])
            sy.dma_start(out=A_sb[:, 0 : nD * N], in_=src).then_inc(m_ld, 16)
            sy.dma_start(out=dtb_sb[:], in_=bass.AP(dtb.tensor, 0, [[1, 128], [128, nD]]), allow_slow_non_contiguous=True).then_inc(m_ld, 16)
            sy.dma_start(out=Dv_sb[:], in_=bass.AP(D_vec.tensor, 0, [[1, 128], [128, nD]]), allow_slow_non_contiguous=True).then_inc(m_ld, 16)
            sy.dma_start(out=ident[:], in_=ident_in[:]).then_inc(m_ld, 16)
            sy.dma_start(out=fb[:], in_=xpw[0:128, :]).then_inc(m_ld, 16)
            sy.dma_start(out=fc[0:32, :], in_=xpw[128:XP, :]).then_inc(m_ld, 16)
            C["ld"] += 6 * 16

        @blk.scalar
        def _(s):
            s.add_instruction(mybir.InstLoadActFuncSet(
                name=nc.get_next_instruction_name(), act_func_set_id=6))
            s.wait_ge(m_ld, C["ld"])
            s.activation(fa[:, 0 : nD * N], A_sb[:, 0 : nD * N], AF.Exp)
            sdone(s)

        @blk.vector
        def _(v):
            v.wait_ge(m_s, C["s"])
            v.tensor_scalar_mul(A_sb[:, 0 : nD * N], fa[:, 0 : nD * N], -1.0)
            v.drain()
            v.tensor_copy(slab[:, 0:D], fb[:])
            v.tensor_copy(slab[0:32, D : 2 * D], fc[0:32, :])
            vdone(v)

        for dt in range(nD):
            @blk.sync
            def _(sy, dt=dt):
                sy.wait_ge(m_v, C["v"])
                sy.dma_start(out=fb[:, 0:R], in_=dtw[dt * 128 : (dt + 1) * 128, :]).then_inc(m_ld, 16)
                C["ld"] += 16
            @blk.vector
            def _(v, dt=dt):
                v.wait_ge(m_ld, C["ld"])
                v.tensor_copy(slab[:, 2 * D + dt * 128 : 2 * D + (dt + 1) * 128], fb[:, 0:R])
                vdone(v)

        for dt in range(nD):
            @blk.tensor
            def _(t, dt=dt):
                t.wait_ge(m_v, C["v"])
                t.matmul(ps_tr[:, 0:128], slab[:, dt * 128 : (dt + 1) * 128], ident[:],
                         is_transpose=True, start=True, stop=True)
                t.matmul(ps_tr[:, 128:160], slab[0:32, D + dt * 128 : D + (dt + 1) * 128],
                         ident[0:32, 0:32], is_transpose=True, start=True, stop=True)
                i = t.matmul(ps_tr[:, 256:384], slab[:, 2 * D + dt * 128 : 2 * D + (dt + 1) * 128],
                             ident[:], is_transpose=True, start=True, stop=True)
                inc(i, "t")
            @blk.vector
            def _(v, dt=dt):
                v.wait_ge(m_t, C["t"])
                v.tensor_copy(xpwT[:, dt * XP : dt * XP + 128], ps_tr[:, 0:128])
                v.tensor_copy(xpwT[:, dt * XP + 128 : dt * XP + XP], ps_tr[:, 128:160])
                v.tensor_copy(slab[:, o_dtwT + dt * 128 : o_dtwT + (dt + 1) * 128], ps_tr[:, 256:384])
                vdone(v)

        o_pwa = SLAB - 2 * D
        o_pwb = SLAB - D
        pw_s_snap = {}

        def emit_pw(jt):
            wb = fb if jt % 2 == 0 else fc
            sp = (jt % 2) * 8
            @blk.sync
            def _(sy, jt=jt, wb=wb):
                if jt < 2:
                    sy.wait_ge(m_v, C["v"])
                else:
                    sy.wait_ge(m_s2, pw_s_snap[jt - 2])
                wck = D // 8
                for ck in range(8):
                    i = sy.dma_start(out=wb[:, ck * wck : (ck + 1) * wck],
                                     in_=W[jt * 128 : (jt + 1) * 128,
                                           ck * wck : (ck + 1) * wck])
                    inc(i, "l2", 16)
            @blk.vector
            def _(v, jt=jt, wb=wb, sp=sp):
                v.wait_ge(m_l2, C["l2"])
                v.wait_ge(m_s2, C["s2"])
                if jt > 1:
                    v.wait_ge(m_st2, C["st2"])
                i = v.reduce_max(st[:, sp : sp + 1], wb[:], axis=AX.X,
                                 apply_absolute_value=True)
                i.then_inc(m_v2, 1)
                C["v2"] += 1
                v.wait_ge(m_v2, C["v2"])
                i = v.reciprocal(st[:, sp + 1 : sp + 2], st[:, sp : sp + 1])
                i.then_inc(m_v2, 1)
                C["v2"] += 1
                v.wait_ge(m_v2, C["v2"])
                v.tensor_scalar_mul(st[:, sp + 2 : sp + 3], st[:, sp + 1 : sp + 2], 127.0)
                i = v.tensor_scalar_mul(st[:, sp + 3 : sp + 4], st[:, sp : sp + 1], 1.0 / 127.0)
                i.then_inc(m_v2, 1)
                C["v2"] += 1
            @blk.scalar
            def _(s, jt=jt, wb=wb, sp=sp):
                s.wait_ge(m_v2, C["v2"])
                i = s.activation(wsrep[:], wb[:], AF.Copy, scale=st[:, sp + 2 : sp + 3])
                i.then_inc(m_s2, 1)
                C["s2"] += 1
                pw_s_snap[jt] = C["s2"]
            @blk.vector
            def _(v):
                v.wait_ge(m_s2, C["s2"])
                v.wait_ge(m_st2, C["st2"])
                def vf2(i):
                    i.then_inc(m_v2, 1)
                    C["v2"] += 1
                vf2(v.tensor_copy(iq[:], wsrep[:, 0 : D // 2]))
                v.wait_ge(m_v2, C["v2"])
                vf2(v.tensor_copy(slab[:, o_pwa : o_pwa + D // 2], iq[:]))
                v.wait_ge(m_v2, C["v2"])
                vf2(v.tensor_copy(iq[:], wsrep[:, D // 2 : D]))
                v.wait_ge(m_v2, C["v2"])
                vf2(v.tensor_copy(slab[:, o_pwa + D // 2 : o_pwa + D], iq[:]))
            for g in range(max(1, nD // 8)):
                GK = min(8, nD)
                @blk.tensor
                def _(t, g=g, GK=GK):
                    t.wait_ge(m_v2, C["v2"])
                    for k in range(GK):
                        dt = g * GK + k
                        i = t.matmul(ps_tr[:, 1024 + k * 128 : 1024 + (k + 1) * 128],
                                     slab[:, o_pwa + dt * 128 : o_pwa + (dt + 1) * 128],
                                     ident[:], is_transpose=True, start=True, stop=True)
                    inc(i, "t2")
                @blk.vector
                def _(v, g=g, GK=GK):
                    v.wait_ge(m_t2, C["t2"])
                    dst = bass.AP(slab, o_pwb + (GK * g) * 128,
                                  [[SLAB, 128], [128, GK], [1, 128]])
                    i = v.tensor_copy(dst, ps_tr[:, 1024 : 1024 + GK * 128])
                    i.then_inc(m_v2, 1)
                    C["v2"] += 1
            @blk.sync
            def _(sy):
                sy.wait_ge(m_v2, C["v2"])
                i = sy.dma_start(
                    out=bass.AP(wq_hbm.tensor, jt * 128,
                                [[nJ * 128, 128], [128 * nJ * 128, nD], [1, 128]]),
                    in_=bass.AP(slab, o_pwb, [[SLAB, 128], [128, nD], [1, 128]]))
                inc(i, "st2", 16)
                i = sy.dma_start(out=bass.AP(wsc_hbm.tensor, jt * 128, [[1, 128], [1, 1]]),
                                 in_=st[:, sp + 3 : sp + 4])
                inc(i, "st2", 16)

        xrow_b = slab[:, 0:D]
        pa1_v_snap = {}

        def emit_pa1(tt):
            xb = fa if tt % 2 == 0 else fd
            @blk.sync
            def _(sy, tt=tt, xb=xb):
                if tt == 0:
                    sy.wait_ge(m_v, C["v"])
                elif tt >= 2:
                    sy.wait_ge(m_v, pa1_v_snap[tt - 2])
                xck = D // 4
                for ck in range(4):
                    i = sy.dma_start(out=xb[:, ck * xck : (ck + 1) * xck],
                                     in_=x_all[tt * 128 : (tt + 1) * 128,
                                               ck * xck : (ck + 1) * xck])
                    inc(i, "ld", 16)
            @blk.vector
            def _(v, tt=tt, xb=xb):
                v.wait_ge(m_ld, C["ld"])
                v.wait_ge(m_st, C["st"])
                vfin(v.tensor_copy(xrow_b, xb[:]))
                pa1_v_snap[tt] = C["v"]
            for g in range(max(1, nD // 8)):
                GK = min(8, nD)
                @blk.tensor
                def _(t, g=g, GK=GK):
                    t.wait_ge(m_v, C["v"])
                    for k in range(GK):
                        dt = g * GK + k
                        i = t.matmul(ps_tr[:, k * 128 : (k + 1) * 128],
                                     slab[:, dt * 128 : (dt + 1) * 128], ident[:],
                                     is_transpose=True, start=True, stop=True)
                    inc(i, "t")
                @blk.vector
                def _(v, g=g, GK=GK):
                    v.wait_ge(m_t, C["t"])
                    dst = bass.AP(slab, D + (GK * g) * 128,
                                  [[SLAB, 128], [128, GK], [1, 128]])
                    vfin(v.tensor_copy(dst, ps_tr[:, 0 : GK * 128]))
            @blk.sync
            def _(sy):
                sy.wait_ge(m_v, C["v"])
                i = sy.dma_start(
                    out=bass.AP(xT_hbm.tensor, tt * 128,
                                [[T, 128], [128 * T, nD], [1, 128]]),
                    in_=bass.AP(slab, D, [[SLAB, 128], [128, nD], [1, 128]]))
                inc(i, "st", 16)

        for i in range(nTc):
            emit_pa1(i)
            for jt in range(i * nJ // nTc, (i + 1) * nJ // nTc):
                emit_pw(jt)

        ld_snap = {}
        for sweep in range(2):
            for dt in range(nD):
                o_xt = (dt % 2) * T
                @blk.sync
                def _(sy, sweep=sweep, dt=dt, o_xt=o_xt):
                    if sweep == 0 and dt == 0:
                        sy.wait_ge(m_st, C["st"])
                        sy.wait_ge(m_v, C["v"])
                    k = (sweep, dt)
                    kprev = (sweep, dt - 2) if dt >= 2 else ((sweep - 1, nD - 2 + dt) if sweep > 0 else None)
                    if kprev is not None and kprev in ld_snap:
                        sy.wait_ge(m_t, ld_snap[kprev])
                    i = sy.dma_start(
                        out=slab[:, o_xt : o_xt + T],
                        in_=bass.AP(xT_hbm.tensor, dt * 128 * T, [[T, 128], [1, T]]))
                    inc(i, "l%d" % (dt % 2), 16)
                @blk.tensor
                def _(t, sweep=sweep, dt=dt, o_xt=o_xt):
                    t.wait_ge(m_l[dt % 2], C["l%d" % (dt % 2)])
                    if sweep == 1 and dt == 0:
                        t.wait_ge(m_v, C["v"])
                    for tc in range(T // 512):
                        if sweep == 0:
                            i = t.matmul(ps_big[:, tc * 512 : (tc + 1) * 512],
                                         xpwT[:, dt * XP : dt * XP + 128],
                                         slab[:, o_xt + tc * 512 : o_xt + (tc + 1) * 512],
                                         start=(dt == 0), stop=(dt == nD - 1))
                        else:
                            i = t.matmul(ps_big[0:32, tc * 512 : (tc + 1) * 512],
                                         xpwT[:, dt * XP + 128 : dt * XP + XP],
                                         slab[:, o_xt + tc * 512 : o_xt + (tc + 1) * 512],
                                         start=(dt == 0), stop=(dt == nD - 1))
                    inc(i, "t")
                    ld_snap[(sweep, dt)] = C["t"]
            if sweep == 0:
                @blk.vector
                def _(v):
                    v.wait_ge(m_t, C["t"])
                    v.wait_ge(m_s2, C["s2"])
                    v.tensor_copy(fb[:, 0:T], ps_big[:, 0:T])
                    vdone(v)
            else:
                @blk.vector
                def _(v):
                    v.wait_ge(m_t, C["t"])
                    v.wait_ge(m_s2, C["s2"])
                    v.tensor_copy(fc[0:32, 0:T], ps_big[0:32, 0:T])
                    vdone(v)
        @blk.vector
        def _(v):
            v.tensor_copy(slab[:, 2 * T : 2 * T + T], fb[:, 0:T])
            v.tensor_copy(slab[0:32, o_bc : o_bc + T], fc[0:32, 0:T])
            vdone(v)
        @blk.sync
        def _(sy):
            sy.wait_ge(m_v, C["v"])
            i = sy.dma_start(out=bc_hbm[:], in_=slab[0:32, o_bc : o_bc + T])
            inc(i, "st", 16)

        o_xpb = 2 * T
        s_snap = {}
        v_snap = {}
        for dt in range(nD):
            o_xt = (dt % 2) * T
            @blk.sync
            def _(sy, dt=dt, o_xt=o_xt):
                if dt < 2:
                    sy.wait_ge(m_t, C["t"])
                if dt >= 2:
                    sy.wait_ge(m_v, v_snap[dt - 2])
                i = sy.dma_start(
                    out=slab[:, o_xt : o_xt + T],
                    in_=bass.AP(xT_hbm.tensor, dt * 128 * T, [[T, 128], [1, T]]))
                inc(i, "l%d" % (dt % 2), 16)
            @blk.tensor
            def _(t, dt=dt):
                if dt == 0:
                    t.wait_ge(m_v, C["v"])
                if dt >= 1:
                    t.wait_ge(m_s, s_snap[dt - 1])
                for tc in range(T // 512):
                    i = t.matmul(ps_big[:, tc * 512 : (tc + 1) * 512],
                                 slab[:, o_dtwT + dt * 128 : o_dtwT + (dt + 1) * 128],
                                 slab[:, o_xpb + tc * 512 : o_xpb + (tc + 1) * 512],
                                 start=True, stop=True)
                inc(i, "t")
            @blk.scalar
            def _(s, dt=dt):
                s.wait_ge(m_t, C["t"])
                if dt >= 1:
                    s.wait_ge(m_st, C["st"])
                    s.wait_ge(m_v, C["v"])
                if dt >= 1:
                    s.wait_ge(m_s, C["s"])
                sfin(s.activation(fa[:, 0:T], ps_big[:, 0:T], AF.Exp,
                                  bias=dtb_sb[:, dt : dt + 1]))
                s.wait_ge(m_s, C["s"])
                sfin(s.activation(slab[:, 2 * T + T : 2 * T + T + T], fa[:, 0:T],
                                  AF.Ln, bias=1.0))
                s_snap[dt] = C["s"]
            @blk.vector
            def _(v, dt=dt, o_xt=o_xt):
                v.wait_ge(m_l[dt % 2], C["l%d" % (dt % 2)])
                v.wait_ge(m_s, C["s"])
                if dt >= 1:
                    v.wait_ge(m_st, C["st"])
                o_dtt = 2 * T + T
                o_ut = 2 * T + 2 * T
                vfin(v.tensor_tensor(slab[:, o_ut : o_ut + T], slab[:, o_dtt : o_dtt + T],
                                     slab[:, o_xt : o_xt + T], OP.mult))
                vfin(v.tensor_scalar(slab[:, o_ut + T : o_ut + T + L],
                                     slab[:, o_xt + P : o_xt + T],
                                     Dv_sb[:, dt : dt + 1], None, OP.mult))
                v_snap[dt] = C["v"]
            @blk.sync
            def _(sy, dt=dt):
                sy.wait_ge(m_v, C["v"])
                o_dtt = 2 * T + T
                o_ut = 2 * T + 2 * T
                i = sy.dma_start(out=bass.AP(dt_hbm.tensor, dt * 128 * T, [[T, 128], [1, T]]),
                                 in_=slab[:, o_dtt : o_dtt + T])
                inc(i, "st", 16)
                i = sy.dma_start(out=bass.AP(u_hbm.tensor, dt * 128 * T, [[T, 128], [1, T]]),
                                 in_=slab[:, o_ut : o_ut + T])
                inc(i, "st", 16)
                i = sy.dma_start(out=bass.AP(gd_hbm.tensor, dt * 128 * L, [[L, 128], [1, L]]),
                                 in_=slab[:, o_ut + T : o_ut + T + L])
                inc(i, "st", 16)

        NQ = 4
        o_br = 0
        o_cr = o_br + N * P
        o_dA = o_cr + N * L
        o_dbx = o_dA + 8 * P
        o_h = o_dbx + 2 * P
        o_g = o_h + 2 * P
        o_dtu = o_g + 4 * L
        o_gd = o_dtu + 4 * P
        o_ones = o_gd + L
        assert o_ones + P <= SLAB

        ps_v_snap = {}
        ps_s_snap = {}
        ps_evac_snap = {}
        @blk.vector
        def _(v):
            v.wait_ge(m_st, C["st"])
            v.wait_ge(m_st2, C["st2"])
            vfin(v.memset(slab[:, o_ones : o_ones + P], 1.0))
        ps_v0 = C["v"]

        for ch in range(2):
            t0 = ch * P
            @blk.sync
            def _(sy, ch=ch, t0=t0):
                sy.wait_ge(m_v, C["v"])
                sy.wait_ge(m_v2, C["v2"])
                sy.wait_ge(m_st, C["st"])
                sy.wait_ge(m_st2, C["st2"])
                if DBG:
                    sy.wait_ge(m_o, C["o"])
                for n in range(N):
                    i = sy.dma_start(
                        out=slab[:, o_br + n * P : o_br + (n + 1) * P],
                        in_=bass.AP(bc_hbm.tensor, n * T + t0, [[0, 128], [1, P]]))
                    inc(i, "ld", 16)
                if ch == 1:
                    for n in range(N):
                        i = sy.dma_start(
                            out=slab[:, o_cr + n * L : o_cr + (n + 1) * L],
                            in_=bass.AP(bc_hbm.tensor, (N + n) * T + P, [[0, 128], [1, L]]))
                        inc(i, "ld", 16)

            for dt in range(nD):
                pyt = py0
                o_dt_ = o_dtu + (dt % 2) * P
                o_ut_ = o_dtu + 2 * P + (dt % 2) * P
                @blk.sync
                def _(sy, ch=ch, dt=dt, t0=t0, o_dt_=o_dt_, o_ut_=o_ut_):
                    key = (ch, dt - 2)
                    if key in ps_s_snap:
                        sy.wait_ge(m_s, ps_s_snap[key])
                        sy.wait_ge(m_v, ps_v_snap[key])
                    elif ch == 1 and dt < 2:
                        sy.wait_ge(m_s, ps_s_snap[(0, nD - 2 + dt)])
                        sy.wait_ge(m_v, ps_v_snap[(0, nD - 2 + dt)])
                    i = sy.dma_start(out=slab[:, o_dt_ : o_dt_ + P],
                                     in_=bass.AP(dt_hbm.tensor, dt * 128 * T + t0,
                                                 [[T, 128], [1, P]]))
                    inc(i, "l%d" % (dt % 2), 16)
                    i = sy.dma_start(out=slab[:, o_ut_ : o_ut_ + P],
                                     in_=bass.AP(u_hbm.tensor, dt * 128 * T + t0,
                                                 [[T, 128], [1, P]]))
                    inc(i, "l%d" % (dt % 2), 16)
                    if ch == 1:
                        if dt >= 1:
                            sy.wait_ge(m_v, ps_evac_snap[dt - 1])
                            sy.wait_ge(m_st, C["st"])
                        i = sy.dma_start(out=slab[:, o_gd : o_gd + L],
                                         in_=bass.AP(gd_hbm.tensor, dt * 128 * L,
                                                     [[L, 128], [1, L]]))
                        inc(i, "l%d" % (dt % 2), 16)

                if ch == 0:
                    @blk.vector
                    def _(v, dt=dt, o_dt_=o_dt_):
                        v.wait_ge(m_l[dt % 2], C["l%d" % (dt % 2)])
                        if dt >= 1:
                            v.wait_ge(m_s, C["s"])
                        vfin(nc.vector.tensor_tensor_scan(
                            fc[:, 0:P], slab[:, o_ones : o_ones + P],
                            slab[:, o_dt_ : o_dt_ + P], 0.0, OP.mult, OP.subtract))
                        v.wait_ge(m_v, C["v"])
                        vfin(v.tensor_scalar(bias16[:], A_sb[:, dt * N : (dt + 1) * N],
                                             fc[:, P - 1 : P], -1.0, OP.mult, OP.mult))
                else:
                    @blk.tensor
                    def _(t, dt=dt, pyt=pyt):
                        t.wait_ge(m_l[dt % 2], C["l%d" % (dt % 2)])
                        if dt >= 1:
                            t.wait_ge(m_v, ps_evac_snap[dt - 1])
                        for q in range(L // QL):
                            i = t.matmul(pyt[:, q * QL : (q + 1) * QL], ident[:],
                                         slab[:, o_gd + q * QL : o_gd + (q + 1) * QL],
                                         start=True, stop=False)
                        inc(i, "t")

                for q0 in range(0, N, NQ):
                    qb = o_dA + ((q0 // NQ) % 2) * NQ * P
                    if ch == 0:
                        @blk.scalar
                        def _(s, dt=dt, q0=q0, qb=qb):
                            if q0 == 0:
                                s.wait_ge(m_v, C["v"])
                            par = (q0 // NQ) % 2
                            for k in range(NQ):
                                n = q0 + k
                                wv = C.get(("dA", par, k), ps_v0)
                                s.wait_ge(m_v, wv)
                                sfin(s.activation(slab[:, qb + k * P : qb + (k + 1) * P],
                                                  fc[:, 0:P], AF.Exp,
                                                  scale=A_sb[:, dt * N + n : dt * N + n + 1],
                                                  bias=bias16[:, n : n + 1]))
                        @blk.vector
                        def _(v, dt=dt, q0=q0, qb=qb, o_ut_=o_ut_):
                            s_last = C["s"]
                            if q0 == 0:
                                v.wait_ge(m_l[dt % 2], C["l%d" % (dt % 2)])
                                v.wait_ge(m_ld, C["ld"])
                            ack = {}
                            def emit(tag, wait_tags, fn):
                                for w in wait_tags:
                                    wv = ack.get(w) if isinstance(w, str) else w
                                    if wv:
                                        v.wait_ge(m_v, wv)
                                i = fn()
                                i.then_inc(m_v, 1)
                                C["v"] += 1
                                ack[tag] = C["v"]
                            def B(k):
                                n = q0 + k
                                pb_ = o_dbx + (k % 2) * P
                                return v.tensor_tensor(slab[:, pb_ : pb_ + P],
                                                       slab[:, o_ut_ : o_ut_ + P],
                                                       slab[:, o_br + n * P : o_br + (n + 1) * P],
                                                       OP.mult)
                            def ST(k):
                                n = q0 + k
                                pb_ = o_dbx + (k % 2) * P
                                ph_ = o_h + (k % 2) * P
                                return nc.vector.scalar_tensor_tensor(
                                    slab[:, ph_ : ph_ + P], slab[:, pb_ : pb_ + P],
                                    1.0, slab[:, qb + k * P : qb + (k + 1) * P],
                                    OP.mult, OP.mult,
                                    accum_out=carry[:, dt * N + n : dt * N + n + 1])
                            def STW(k):
                                v.wait_ge(m_s, s_last - (NQ - 1 - k))
                                return ST(k)
                            emit("B0", [C.get("pbx0", 0)], lambda: B(0))
                            emit("B1", [C.get("pbx1", 0)], lambda: B(1))
                            emit("S0", ["B0"], lambda: STW(0))
                            emit("S1", ["B1"], lambda: STW(1))
                            emit("B2", ["S0"], lambda: B(2))
                            emit("B3", ["S1"], lambda: B(3))
                            emit("S2", ["B2", "S0"], lambda: STW(2))
                            emit("S3", ["B3", "S1"], lambda: STW(3))
                            C["pbx0"] = ack["S2"]
                            C["pbx1"] = ack["S3"]
                            par = (q0 // NQ) % 2
                            for k in range(NQ):
                                C[("dA", par, k)] = ack["S%d" % k]
                    else:
                        @blk.scalar
                        def _(s, dt=dt, q0=q0, qb=qb, o_dt_=o_dt_):
                            if q0 == 0:
                                s.wait_ge(m_l[dt % 2], C["l%d" % (dt % 2)])
                                s.wait_ge(m_ld, C["ld"])
                            par = (q0 // NQ) % 2
                            for k in range(NQ):
                                n = q0 + k
                                wv = C.get(("dA", par, k), ps_v0)
                                s.wait_ge(m_v, wv)
                                sfin(s.activation(slab[:, qb + k * P : qb + (k + 1) * P],
                                                  slab[:, o_dt_ : o_dt_ + P], AF.Exp,
                                                  scale=A_sb[:, dt * N + n : dt * N + n + 1]))
                        @blk.vector
                        def _(v, dt=dt, q0=q0, qb=qb, o_ut_=o_ut_):
                            s_last = C["s"]
                            if q0 == 0:
                                v.wait_ge(m_l[dt % 2], C["l%d" % (dt % 2)])
                                v.wait_ge(m_ld, C["ld"])
                            ack = {}
                            def emit(tag, wait_tags, fn):
                                for w in wait_tags:
                                    wv = ack.get(w) if isinstance(w, str) else w
                                    if wv:
                                        v.wait_ge(m_v, wv)
                                i = fn()
                                i.then_inc(m_v, 1)
                                C["v"] += 1
                                ack[tag] = C["v"]
                            def B(k):
                                n = q0 + k
                                pb_ = o_dbx + (k % 2) * P
                                return v.tensor_tensor(slab[:, pb_ : pb_ + P],
                                                       slab[:, o_ut_ : o_ut_ + P],
                                                       slab[:, o_br + n * P : o_br + (n + 1) * P],
                                                       OP.mult)
                            def TT_(k):
                                n = q0 + k
                                pb_ = o_dbx + (k % 2) * P
                                ph_ = o_h + (k % 2) * P
                                return nc.vector.tensor_tensor_scan(
                                    slab[:, ph_ : ph_ + P], slab[:, qb + k * P : qb + (k + 1) * P],
                                    slab[:, pb_ : pb_ + P],
                                    carry[:, dt * N + n : dt * N + n + 1],
                                    OP.mult, OP.add)
                            p_base = C["p"]
                            def TW(k):
                                v.wait_ge(m_s, s_last - (NQ - 1 - k))
                                if k < 2:
                                    wv = C.get("hC%d" % k, 0)
                                    if wv:
                                        v.wait_ge(m_p, wv)
                                else:
                                    v.wait_ge(m_p, p_base + (k - 2) + 1)
                                return TT_(k)
                            emit("B0", [C.get("pbx0", 0)], lambda: B(0))
                            emit("B1", [C.get("pbx1", 0)], lambda: B(1))
                            emit("T0", ["B0"], lambda: TW(0))
                            emit("T1", ["B1"], lambda: TW(1))
                            emit("B2", ["T0"], lambda: B(2))
                            emit("B3", ["T1"], lambda: B(3))
                            emit("T2", ["B2"], lambda: TW(2))
                            emit("T3", ["B3"], lambda: TW(3))
                            C["pbx0"] = ack["T2"]
                            C["pbx1"] = ack["T3"]
                            C["hC0"] = p_base + 3
                            C["hC1"] = p_base + 4
                            par = (q0 // NQ) % 2
                            for k in range(NQ):
                                C[("dA", par, k)] = ack["T%d" % k]
                                C[("tts", k)] = ack["T%d" % k]
                        @blk.gpsimd
                        def _(g, dt=dt, q0=q0):
                            if dt == 0 and q0 == 0:
                                g.wait_ge(m_ld, C["ld"])
                            for k in range(NQ):
                                n = q0 + k
                                ph_ = o_h + (k % 2) * P
                                pg = o_g + k * L
                                g.wait_ge(m_v, C[("tts", k)])
                                wv = C.get(("gt", k), 0)
                                if wv:
                                    g.wait_ge(m_t, wv)
                                i = g.tensor_tensor(slab[:, pg : pg + L],
                                                    slab[:, ph_ : ph_ + P],
                                                    slab[:, o_cr + n * L : o_cr + (n + 1) * L],
                                                    OP.mult)
                                inc(i, "p")
                        @blk.tensor
                        def _(t, dt=dt, q0=q0):
                            for k in range(NQ):
                                n = q0 + k
                                pg = o_g + k * L
                                t.wait_ge(m_p, C["p"] - (NQ - 1 - k))
                                for qq in range(L // QL):
                                    i = t.matmul(py0[:, qq * QL : (qq + 1) * QL], ident[:],
                                                 slab[:, pg + qq * QL : pg + (qq + 1) * QL],
                                                 start=False, stop=(n == N - 1))
                                inc(i, "t")
                                C[("gt", k)] = C["t"]
                ps_s_snap[(ch, dt)] = C["s"]
                ps_v_snap[(ch, dt)] = C["v"]
                if ch == 1:
                    @blk.vector
                    def _(v, dt=dt, pyt=pyt):
                        v.wait_ge(m_t, C["t"])
                        vfin(v.tensor_copy(slab[:, o_gd : o_gd + L], pyt[:, 0:L]))
                        ps_evac_snap[dt] = C["v"]
                    @blk.sync
                    def _(sy, dt=dt):
                        sy.wait_ge(m_v, C["v"])
                        i = sy.dma_start(
                            out=bass.AP(y_hbm.tensor, dt * 128 * L, [[L, 128], [1, L]]),
                            in_=slab[:, o_gd : o_gd + L])
                        inc(i, "st", 16)

        @blk.sync
        def _(sy):
            sy.wait_ge(m_st, C["st"])
            sy.wait_ge(m_st2, C["st2"])
            i = sy.dma_start(out=wsrep[:], in_=bass.AP(wsc_hbm.tensor, 0, [[0, 128], [1, D]]))
            inc(i, "ld", 16)

        o_ysl = 0
        GT0 = min(8, nD)
        o_xq = 2 * GT0 * 128
        o_xqT = o_xq + D
        o_wg = o_xqT + nD * 128 * TTH
        wg_t_snap = {}
        assert o_wg + 2 * nD * JG <= SLAB
        ps_ffn = [ps_big[:, 0:512], ps_big[:, 512:1024],
                  ps_big[:, 1024:1536], ps_big[:, 1536:2048]]

        for half in range(nHALF):
            tts = list(range(half * TTH, (half + 1) * TTH))
            yt_snap = {}
            yld_snap = {}
            for tt in tts:
                @blk.sync
                def _(sy, tt=tt):
                    sy.wait_ge(m_v, C["v"])
                    sy.wait_ge(m_s, C["s"])
                    i = sy.dma_start(out=fa[:], in_=x_all[P + tt * 128 : P + (tt + 1) * 128, :])
                    inc(i, "ld", 16)
                GT = min(8, nD)
                nG = nD // GT
                for g in range(nG):
                    ob = o_ysl + (g % 2) * GT * 128
                    @blk.sync
                    def _(sy, tt=tt, g=g, GT=GT, ob=ob):
                        if (tt, g - 2) in yt_snap:
                            sy.wait_ge(m_t, yt_snap[(tt, g - 2)])
                        dt0 = g * GT
                        i = sy.dma_start(
                            out=bass.AP(slab, ob, [[SLAB, 128], [128, GT], [1, 128]]),
                            in_=bass.AP(y_hbm.tensor, dt0 * 128 * L + tt * 128,
                                        [[L, 128], [128 * L, GT], [1, 128]]))
                        inc(i, "l%d" % (g % 2), 16)
                        yld_snap[(tt, g)] = C["l%d" % (g % 2)]
                    @blk.tensor
                    def _(t, tt=tt, g=g, GT=GT, ob=ob):
                        t.wait_ge(m_l[g % 2], yld_snap[(tt, g)])
                        if g > 0:
                            t.wait_ge(m_v, C["v"])
                        for k in range(GT):
                            i = t.matmul(ps_tr[:, k * 128 : (k + 1) * 128],
                                         slab[:, ob + k * 128 : ob + (k + 1) * 128],
                                         ident[:], is_transpose=True, start=True, stop=True)
                        inc(i, "t")
                        yt_snap[(tt, g)] = C["t"]
                    @blk.vector
                    def _(v, tt=tt, g=g, GT=GT):
                        v.wait_ge(m_t, C["t"])
                        if g == 0:
                            v.wait_ge(m_ld, C["ld"])
                            v.wait_ge(m_st, C["st"])
                            v.wait_ge(m_o, C["o"])
                        vfin(v.tensor_tensor(fb[:, g * GT * 128 : (g + 1) * GT * 128],
                                             ps_tr[:, 0 : GT * 128],
                                             fa[:, g * GT * 128 : (g + 1) * GT * 128],
                                             OP.add))
                @blk.vector
                def _(v, tt=tt):
                    v.wait_ge(m_v, C["v"])
                    vfin(v.reduce_sum(st[:, 4:5], fb[:], axis=AX.X))
                    v.wait_ge(m_v, C["v"])
                    vfin(v.tensor_scalar_mul(st[:, 5:6], st[:, 4:5], 1.0 / D))
                @blk.scalar
                def _(s, tt=tt):
                    s.wait_ge(m_v, C["v"])
                    sfin(s.activation(fc[:], fb[:], AF.Square, accum_out=st[:, 6:7]))
                @blk.vector
                def _(v, tt=tt):
                    v.wait_ge(m_s, C["s"])
                    vfin(v.tensor_scalar_mul(st[:, 7:8], st[:, 6:7], 1.0 / D))
                    vfin(v.tensor_tensor(st[:, 8:9], st[:, 5:6], st[:, 5:6], OP.mult))
                    v.wait_ge(m_v, C["v"])
                    vfin(v.tensor_tensor(st[:, 9:10], st[:, 7:8], st[:, 8:9], OP.subtract))
                    v.wait_ge(m_v, C["v"])
                    vfin(v.tensor_scalar_add(st[:, 9:10], st[:, 9:10], 1e-5))
                @blk.scalar
                def _(s, tt=tt):
                    s.wait_ge(m_v, C["v"])
                    sfin(s.activation(st[:, 10:11], st[:, 9:10], AF.Ln))
                    s.wait_ge(m_s, C["s"])
                    sfin(s.activation(st[:, 11:12], st[:, 10:11], AF.Exp, scale=-0.5))
                @blk.vector
                def _(v, tt=tt):
                    v.wait_ge(m_s, C["s"])
                    vfin(v.tensor_scalar(fb[:], fb[:], st[:, 5:6], st[:, 11:12],
                                         OP.subtract, OP.mult))
                @blk.sync
                def _(sy, tt=tt):
                    sy.wait_ge(m_v, C["v"])
                    i = sy.dma_start(out=bass.AP(xm_hbm.tensor, tt * 128 * D, [[D, 128], [1, D]]),
                                     in_=fb[:])
                    inc(i, "st", 16)
                @blk.vector
                def _(v, tt=tt):
                    v.wait_ge(m_v, C["v"])
                    vfin(v.reduce_max(st[:, 14:15], fb[:], axis=AX.X,
                                      apply_absolute_value=True))
                    v.wait_ge(m_v, C["v"])
                    vfin(v.reciprocal(st[:, 15:16], st[:, 14:15]))
                    v.wait_ge(m_v, C["v"])
                    vfin(v.tensor_scalar_mul(qsc[:, tt : tt + 1], st[:, 15:16], 127.0))
                    vfin(v.tensor_scalar_mul(asc[:, tt : tt + 1], st[:, 14:15], 1.0 / 127.0))
                @blk.scalar
                def _(s, tt=tt):
                    s.wait_ge(m_v, C["v"])
                    sfin(s.activation(fc[:], fb[:], AF.Copy, scale=qsc[:, tt : tt + 1]))
                @blk.vector
                def _(v, tt=tt):
                    v.wait_ge(m_s, C["s"])
                    vfin(v.tensor_copy(iq[:], fc[:, 0 : D // 2]))
                    v.wait_ge(m_v, C["v"])
                    vfin(v.tensor_copy(slab[:, o_xq : o_xq + D // 2], iq[:]))
                    v.wait_ge(m_v, C["v"])
                    vfin(v.tensor_copy(iq[:], fc[:, D // 2 : D]))
                    v.wait_ge(m_v, C["v"])
                    vfin(v.tensor_copy(slab[:, o_xq + D // 2 : o_xq + D], iq[:]))
                for g in range(nD // GT):
                    @blk.tensor
                    def _(t, tt=tt, g=g, GT=GT):
                        t.wait_ge(m_v, C["v"])
                        for k in range(GT):
                            dt = g * GT + k
                            i = t.matmul(ps_tr[:, k * 128 : (k + 1) * 128],
                                         slab[:, o_xq + dt * 128 : o_xq + (dt + 1) * 128],
                                         ident[:], is_transpose=True, start=True, stop=True)
                        inc(i, "t")
                    @blk.vector
                    def _(v, tt=tt, g=g, GT=GT):
                        v.wait_ge(m_t, C["t"])
                        toff = tt - half * TTH
                        dst = bass.AP(slab,
                                      o_xqT + (GT * g) * (128 * TTH) + toff * 128,
                                      [[SLAB, 128], [128 * TTH, GT], [1, 128]])
                        vfin(v.tensor_copy(dst, ps_tr[:, 0 : GT * 128]))
            for jg in range(nJG):
                o_wgb = o_wg + (jg % 2) * nD * JG
                @blk.sync
                def _(sy, jg=jg, o_wgb=o_wgb):
                    if jg >= 2:
                        sy.wait_ge(m_t, wg_t_snap[jg - 2])
                    jt0 = jg * JT_PER_G
                    DCH = min(8, nD)
                    for d0 in range(0, nD, DCH):
                        i = sy.dma_start(
                            out=bass.AP(slab, o_wgb + d0 * JG,
                                        [[SLAB, 128], [JG, DCH], [1, JG]]),
                            in_=bass.AP(wq_hbm.tensor,
                                        d0 * 128 * nJ * 128 + jt0 * 128,
                                        [[nJ * 128, 128], [128 * nJ * 128, DCH],
                                         [1, JG]]))
                        inc(i, "l%d" % (jg % 2), 16)
                for tt in tts:
                    toff = tt - half * TTH
                    @blk.tensor
                    def _(t, jg=jg, tt=tt, toff=toff, o_wgb=o_wgb):
                        t.wait_ge(m_l[jg % 2], C["l%d" % (jg % 2)])
                        t.wait_ge(m_v, C["v"])
                        for dt in range(nD):
                            i = t.matmul(ps_ffn[toff],
                                         slab[:, o_xqT + dt * (128 * TTH) + toff * 128
                                              : o_xqT + dt * (128 * TTH) + (toff + 1) * 128],
                                         slab[:, o_wgb + dt * JG : o_wgb + (dt + 1) * JG],
                                         start=(dt == 0), stop=(dt == nD - 1))
                        inc(i, "t")
                wg_t_snap[jg] = C["t"]
                GB = max(1, min(len(tts), D // 512))
                for b0 in range(0, len(tts), GB):
                    bts = list(range(b0, min(b0 + GB, len(tts))))
                    @blk.vector
                    def _(v, jg=jg, bts=bts):
                        v.wait_ge(m_t, C["t"])
                        v.wait_ge(m_s, C["s"])
                        for j, toff in enumerate(bts):
                            i = v.tensor_tensor(fc[:, j * 512 : (j + 1) * 512], ps_ffn[toff],
                                                wsrep[:, jg * 512 : (jg + 1) * 512], OP.mult)
                        vfin(i)
                    @blk.scalar
                    def _(s, jg=jg, bts=bts):
                        s.wait_ge(m_v, C["v"])
                        s.wait_ge(m_st, C["st"])
                        for j, toff in enumerate(bts):
                            i = s.activation(slab16[:, j * 512 : (j + 1) * 512],
                                             fc[:, j * 512 : (j + 1) * 512], AF.Copy,
                                             scale=asc[:, tts[toff] : tts[toff] + 1])
                        sfin(i)
                    @blk.sync
                    def _(sy, jg=jg, bts=bts):
                        sy.wait_ge(m_s, C["s"])
                        for j, toff in enumerate(bts):
                            i = sy.dma_start(
                                out=bass.AP(f16_hbm.tensor, tts[toff] * 128 * D + jg * 512,
                                            [[D, 128], [1, 512]]),
                                in_=slab16[:, j * 512 : (j + 1) * 512])
                            inc(i, "st", 16)
            sy_snap = {}
            for tt in tts:
                @blk.sync
                def _(sy, tt=tt):
                    sy.wait_ge(m_v, C["v"])
                    sy.wait_ge(m_st, C["st"])
                    i = sy.dma_start(out=fa[:], in_=bass.AP(xm_hbm.tensor, tt * 128 * D,
                                                            [[D, 128], [1, D]]))
                    inc(i, "ld", 16)
                    seg_snap = {}
                    for seg in range(D // 512):
                        fh = f16a if seg % 2 == 0 else f16b
                        if seg >= 2:
                            sy.wait_ge(m_v, C["v"] + seg - 1)
                        i = sy.dma_start(out=fh[:], in_=bass.AP(f16_hbm.tensor,
                                                                tt * 128 * D + seg * 512,
                                                                [[D, 128], [1, 512]]))
                        inc(i, "l%d" % (seg % 2), 16)
                        seg_snap[seg] = C["l%d" % (seg % 2)]
                    sy_snap[tt] = seg_snap
                @blk.vector
                def _(v, tt=tt):
                    v.wait_ge(m_ld, C["ld"])
                    v.wait_ge(m_o, C["o"])
                    for seg in range(D // 512):
                        fh = f16a if seg % 2 == 0 else f16b
                        v.wait_ge(m_l[seg % 2], sy_snap[tt][seg])
                        i = v.tensor_tensor(fb[:, seg * 512 : (seg + 1) * 512],
                                            fa[:, seg * 512 : (seg + 1) * 512], fh[:], OP.add)
                        i.then_inc(m_v, 1)
                        C["v"] += 1
                    v.wait_ge(m_v, C["v"])
                    vfin(v.reduce_sum(st[:, 4:5], fb[:], axis=AX.X))
                    v.wait_ge(m_v, C["v"])
                    vfin(v.tensor_scalar_mul(st[:, 5:6], st[:, 4:5], 1.0 / D))
                @blk.scalar
                def _(s, tt=tt):
                    s.wait_ge(m_v, C["v"])
                    sfin(s.activation(fc[:], fb[:], AF.Square, accum_out=st[:, 6:7]))
                @blk.vector
                def _(v, tt=tt):
                    v.wait_ge(m_s, C["s"])
                    vfin(v.tensor_scalar_mul(st[:, 7:8], st[:, 6:7], 1.0 / D))
                    vfin(v.tensor_tensor(st[:, 8:9], st[:, 5:6], st[:, 5:6], OP.mult))
                    v.wait_ge(m_v, C["v"])
                    vfin(v.tensor_tensor(st[:, 9:10], st[:, 7:8], st[:, 8:9], OP.subtract))
                    v.wait_ge(m_v, C["v"])
                    vfin(v.tensor_scalar_add(st[:, 9:10], st[:, 9:10], 1e-5))
                @blk.scalar
                def _(s, tt=tt):
                    s.wait_ge(m_v, C["v"])
                    sfin(s.activation(st[:, 10:11], st[:, 9:10], AF.Ln))
                    s.wait_ge(m_s, C["s"])
                    sfin(s.activation(st[:, 11:12], st[:, 10:11], AF.Exp, scale=-0.5))
                @blk.vector
                def _(v, tt=tt):
                    v.wait_ge(m_s, C["s"])
                    vfin(v.tensor_scalar(fb[:], fb[:], st[:, 5:6], st[:, 11:12],
                                         OP.subtract, OP.mult))
                @blk.sync
                def _(sy, tt=tt):
                    sy.wait_ge(m_v, C["v"])
                    i = sy.dma_start(out=out_hbm[tt * 128 : (tt + 1) * 128, :], in_=fb[:])
                    inc(i, "o", 16)

        @blk.gpsimd
        def _(g):
            g.wait_ge(m_o, C["o"])

    return nc


_NC_CACHE = {}


def kernel(L0_attn_x, L0_attn_A_log, L0_attn_D, L0_attn_x_proj_w,
           L0_attn_dt_proj_w, L0_attn_dt_proj_b, L0_ffn_weight):
    import ml_dtypes
    x = np.ascontiguousarray(np.asarray(L0_attn_x, dtype=np.float32))
    B, Tfull, Dm = x.shape
    N = np.asarray(L0_attn_A_log).shape[1]
    R = np.asarray(L0_attn_dt_proj_w).shape[1]
    P = Tfull // 2
    L = Tfull // 2

    key = (Dm, P, L, N, R)
    if key not in _NC_CACHE:
        _NC_CACHE[key] = _build(Dm, P, L, N, R)
    nc = _NC_CACHE[key]

    A_log = np.ascontiguousarray(np.asarray(L0_attn_A_log, np.float32))
    Dv = np.ascontiguousarray(np.asarray(L0_attn_D, np.float32))
    xpw = np.ascontiguousarray(np.asarray(L0_attn_x_proj_w, np.float32))
    dtw = np.ascontiguousarray(np.asarray(L0_attn_dt_proj_w, np.float32))
    dtbv = np.ascontiguousarray(np.asarray(L0_attn_dt_proj_b, np.float32))
    Wm = np.ascontiguousarray(np.asarray(L0_ffn_weight, np.float32))
    ident = np.eye(128, dtype=np.float32).astype(ml_dtypes.bfloat16)

    in_maps = []
    for c in range(8):
        b, h = c // 2, c % 2
        if h == 0:
            x_pre = np.zeros((P, Dm), np.float32)
            x_own = x[b, 0:L]
        else:
            x_pre = x[b, 0:P]
            x_own = x[b, P : P + L]
        x_allc = np.ascontiguousarray(np.concatenate([x_pre, x_own], axis=0))
        in_maps.append({
            "x_all": x_allc, "A_log": A_log, "D_vec": Dv, "xpw": xpw,
            "dtw": dtw, "dtb": dtbv, "W": Wm, "ident": ident,
        })

    res = run_bass_kernel_spmd(nc, in_maps, core_ids=list(range(8)))
    out = np.empty((B, Tfull, Dm), np.float32)
    for c in range(8):
        b, h = c // 2, c % 2
        out[b, h * L : (h + 1) * L] = res.results[c]["out"]
    return out


# version 26
# speedup vs baseline: 1.0003x; 1.0003x over previous
import os
import numpy as np
import concourse.bass as bass
import concourse.mybir as mybir
from concourse.bass_utils import run_bass_kernel_spmd

F32 = mybir.dt.float32
BF16 = mybir.dt.bfloat16
F16 = mybir.dt.float16
I32 = mybir.dt.int32
AF = mybir.ActivationFunctionType
OP = mybir.AluOpType
AX = mybir.AxisListType


def _build(D, P, L, N, R):
    assert R == 128 and D % 512 == 0 and P % 128 == 0 and P == L
    QL = 512 if L % 512 == 0 else L
    T = P + L
    nD = D // 128
    nJ = D // 128
    nTT = L // 128
    nTc = T // 128
    XP = R + 2 * N
    JG = 512
    nJG = D // JG
    JT_PER_G = JG // 128
    TTH = min(4, nTT)
    nHALF = nTT // TTH

    nc = bass.Bass("TRN2", target_bir_lowering=False)

    x_all = nc.dram_tensor("x_all", [T, D], F32, kind="ExternalInput").ap()
    A_log = nc.dram_tensor("A_log", [D, N], F32, kind="ExternalInput").ap()
    D_vec = nc.dram_tensor("D_vec", [D], F32, kind="ExternalInput").ap()
    xpw = nc.dram_tensor("xpw", [XP, D], F32, kind="ExternalInput").ap()
    dtw = nc.dram_tensor("dtw", [D, R], F32, kind="ExternalInput").ap()
    dtb = nc.dram_tensor("dtb", [D], F32, kind="ExternalInput").ap()
    W = nc.dram_tensor("W", [D, D], F32, kind="ExternalInput").ap()
    ident_in = nc.dram_tensor("ident", [128, 128], BF16, kind="ExternalInput").ap()
    out_hbm = nc.dram_tensor("out", [L, D], F32, kind="ExternalOutput").ap()

    wq_hbm = nc.dram_tensor("wq_st", [nD, 128, nJ * 128], BF16).ap()
    wsc_hbm = nc.dram_tensor("wsc_st", [D], F32).ap()
    xT_hbm = nc.dram_tensor("xT_st", [nD, 128, T], BF16).ap()
    dt_hbm = nc.dram_tensor("dt_st", [nD, 128, T], BF16).ap()
    u_hbm = nc.dram_tensor("u_st", [nD, 128, T], BF16).ap()
    gd_hbm = nc.dram_tensor("gd_st", [nD, 128, L], BF16).ap()
    bc_hbm = nc.dram_tensor("bc_st", [2 * N, T], BF16).ap()
    y_hbm = nc.dram_tensor("y_st", [nD, 128, L], BF16).ap()
    xm_hbm = nc.dram_tensor("xm_st", [nTT, 128, D], F32).ap()
    f16_hbm = nc.dram_tensor("f16_st", [nTT, 128, D], F16).ap()
    DBG = os.environ.get("K_DEBUG") == "1"
    if DBG:
        dbg_scum = nc.dram_tensor("dbg_scum", [128, T], F32, kind="ExternalOutput").ap()
        dbg_bias = nc.dram_tensor("dbg_bias", [128, 16], F32, kind="ExternalOutput").ap()
        dbg_dA = nc.dram_tensor("dbg_dA", [128, T], BF16, kind="ExternalOutput").ap()
        dbg_dbx = nc.dram_tensor("dbg_dbx", [128, T], BF16, kind="ExternalOutput").ap()
        dbg_br = nc.dram_tensor("dbg_br", [128, T], BF16, kind="ExternalOutput").ap()
        dbg_carry = nc.dram_tensor("dbg_carry", [128, 512], F32, kind="ExternalOutput").ap()

    SLAB = max(2 * N * P + 8 * P + 2 * P + 2 * P + 4 * L + 4 * P + L + P,
               3 * D + T,
               2 * T + T + 2 * T,
               512 + D + nD * 128 * TTH + nD * JG)
    SLAB = max(SLAB, 4 * D, max(5 * T + L, 3 * D) + nD * 128 + T)

    from contextlib import ExitStack
    with ExitStack() as _es:
        def _e(cm):
            return _es.enter_context(cm)
        slab = _e(nc.sbuf_tensor("slab", [128, SLAB], BF16))
        fa = _e(nc.sbuf_tensor("fa", [128, D], F32))
        fb = _e(nc.sbuf_tensor("fb", [128, D], F32))
        fc = _e(nc.sbuf_tensor("fc", [128, D], F32))
        fd = _e(nc.sbuf_tensor("fd", [128, D], F32))
        iq = _e(nc.sbuf_tensor("iq", [128, D // 2], I32))
        wsrep = _e(nc.sbuf_tensor("wsrep", [128, D], F32))
        A_sb = _e(nc.sbuf_tensor("A_sb", [128, nD * N], F32))
        dtb_sb = _e(nc.sbuf_tensor("dtb_sb", [128, nD], F32))
        Dv_sb = _e(nc.sbuf_tensor("Dv_sb", [128, nD], F32))

        ident = _e(nc.sbuf_tensor("ident_sb", [128, 128], BF16))
        
        
        carry = _e(nc.sbuf_tensor("carry", [128, nD * N], F32))
        st = _e(nc.sbuf_tensor("st", [128, 16], F32))
        asc = _e(nc.sbuf_tensor("asc", [128, nTT], F32))
        qsc = _e(nc.sbuf_tensor("qsc", [128, nTT], F32))
        bias16 = _e(nc.sbuf_tensor("bias16", [128, 16], F32))
        f16a = _e(nc.sbuf_tensor("f16a", [128, 512], F16))
        f16b = _e(nc.sbuf_tensor("f16b", [128, 512], F16))
        slab16 = _e(nc.sbuf_tensor("slab16", [128, 4 * 512], F16))
        ps_big = _e(nc.psum_tensor("ps_big", [128, 2048], F32))
        py0 = _e(nc.psum_tensor("py0", [128, 1024], F32))
        ps_tr = _e(nc.psum_tensor("ps_tr", [128, 2048], BF16))
        m_ld = _e(nc.semaphore("m_ld"))
        m_st = _e(nc.semaphore("m_st"))
        m_v = _e(nc.semaphore("m_v"))
        m_s = _e(nc.semaphore("m_s"))
        m_t = _e(nc.semaphore("m_t"))
        m_o = _e(nc.semaphore("m_o"))
        m_l0 = _e(nc.semaphore("m_l0"))
        m_l1 = _e(nc.semaphore("m_l1"))
        m_v2 = _e(nc.semaphore("m_v2"))
        m_s2 = _e(nc.semaphore("m_s2"))
        m_t2 = _e(nc.semaphore("m_t2"))
        m_st2 = _e(nc.semaphore("m_st2"))
        m_l2 = _e(nc.semaphore("m_l2"))
        blk = _e(nc.Block())
        o_dtwT = max(5 * T + L, 3 * D)
        dtwT = slab[:, o_dtwT : o_dtwT + nD * 128]
        o_bc = o_dtwT + nD * 128
        bc_sb = slab[:, o_bc : o_bc + T]
        o_xpwT = o_bc + T
        xpwT = slab[:, o_xpwT : o_xpwT + nD * XP]
        assert o_xpwT + nD * XP <= SLAB - 2 * D
        C = {"ld": 0, "st": 0, "v": 0, "s": 0, "t": 0, "o": 0, "l0": 0, "l1": 0,
             "v2": 0, "s2": 0, "t2": 0, "st2": 0, "l2": 0}
        SEM = {"ld": m_ld, "st": m_st, "v": m_v, "s": m_s, "t": m_t, "o": m_o,
               "l0": m_l0, "l1": m_l1, "v2": m_v2, "s2": m_s2, "t2": m_t2,
               "st2": m_st2, "l2": m_l2}
        m_l = [m_l0, m_l1]

        def inc(instr, key, n=1):
            instr.then_inc(SEM[key], n)
            C[key] += n

        def vdone(v):
            v.drain()
            v.sem_inc(m_v, 1)
            C["v"] += 1

        def sdone(s):
            s.drain()
            s.sem_inc(m_s, 1)
            C["s"] += 1

        def vfin(instr):
            instr.then_inc(m_v, 1)
            C["v"] += 1

        def sfin(instr):
            instr.then_inc(m_s, 1)
            C["s"] += 1

        @blk.sync
        def _(sy):
            src = bass.AP(A_log.tensor, 0, [[N, 128], [128 * N, nD], [1, N]])
            sy.dma_start(out=A_sb[:, 0 : nD * N], in_=src).then_inc(m_ld, 16)
            sy.dma_start(out=dtb_sb[:], in_=bass.AP(dtb.tensor, 0, [[1, 128], [128, nD]]), allow_slow_non_contiguous=True).then_inc(m_ld, 16)
            sy.dma_start(out=Dv_sb[:], in_=bass.AP(D_vec.tensor, 0, [[1, 128], [128, nD]]), allow_slow_non_contiguous=True).then_inc(m_ld, 16)
            sy.dma_start(out=ident[:], in_=ident_in[:]).then_inc(m_ld, 16)
            sy.dma_start(out=fb[:], in_=xpw[0:128, :]).then_inc(m_ld, 16)
            sy.dma_start(out=fc[0:32, :], in_=xpw[128:XP, :]).then_inc(m_ld, 16)
            C["ld"] += 6 * 16

        @blk.scalar
        def _(s):
            s.add_instruction(mybir.InstLoadActFuncSet(
                name=nc.get_next_instruction_name(), act_func_set_id=6))
            s.wait_ge(m_ld, C["ld"])
            s.activation(fa[:, 0 : nD * N], A_sb[:, 0 : nD * N], AF.Exp)
            sdone(s)

        @blk.vector
        def _(v):
            v.wait_ge(m_s, C["s"])
            v.tensor_scalar_mul(A_sb[:, 0 : nD * N], fa[:, 0 : nD * N], -1.0)
            v.drain()
            v.tensor_copy(slab[:, 0:D], fb[:])
            v.tensor_copy(slab[0:32, D : 2 * D], fc[0:32, :])
            vdone(v)

        for dt in range(nD):
            @blk.sync
            def _(sy, dt=dt):
                sy.wait_ge(m_v, C["v"])
                sy.dma_start(out=fb[:, 0:R], in_=dtw[dt * 128 : (dt + 1) * 128, :]).then_inc(m_ld, 16)
                C["ld"] += 16
            @blk.vector
            def _(v, dt=dt):
                v.wait_ge(m_ld, C["ld"])
                v.tensor_copy(slab[:, 2 * D + dt * 128 : 2 * D + (dt + 1) * 128], fb[:, 0:R])
                vdone(v)

        for dt in range(nD):
            @blk.tensor
            def _(t, dt=dt):
                t.wait_ge(m_v, C["v"])
                t.matmul(ps_tr[:, 0:128], slab[:, dt * 128 : (dt + 1) * 128], ident[:],
                         is_transpose=True, start=True, stop=True)
                t.matmul(ps_tr[:, 128:160], slab[0:32, D + dt * 128 : D + (dt + 1) * 128],
                         ident[0:32, 0:32], is_transpose=True, start=True, stop=True)
                i = t.matmul(ps_tr[:, 256:384], slab[:, 2 * D + dt * 128 : 2 * D + (dt + 1) * 128],
                             ident[:], is_transpose=True, start=True, stop=True)
                inc(i, "t")
            @blk.vector
            def _(v, dt=dt):
                v.wait_ge(m_t, C["t"])
                v.tensor_copy(xpwT[:, dt * XP : dt * XP + 128], ps_tr[:, 0:128])
                v.tensor_copy(xpwT[:, dt * XP + 128 : dt * XP + XP], ps_tr[:, 128:160])
                v.tensor_copy(slab[:, o_dtwT + dt * 128 : o_dtwT + (dt + 1) * 128], ps_tr[:, 256:384])
                vdone(v)

        o_pwa = SLAB - 2 * D
        o_pwb = SLAB - D
        pw_s_snap = {}

        def emit_pw(jt):
            wb = fb if jt % 2 == 0 else fc
            sp = (jt % 2) * 8
            @blk.scalar
            def _(sy, jt=jt, wb=wb):
                if jt < 2:
                    sy.wait_ge(m_v, C["v"])
                else:
                    sy.wait_ge(m_s2, pw_s_snap[jt - 2])
                wck = D // 16
                for ck in range(16):
                    i = sy.dma_start(out=wb[:, ck * wck : (ck + 1) * wck],
                                     in_=W[jt * 128 : (jt + 1) * 128,
                                           ck * wck : (ck + 1) * wck])
                    inc(i, "l2", 16)
            @blk.vector
            def _(v, jt=jt, wb=wb, sp=sp):
                v.wait_ge(m_l2, C["l2"])
                v.wait_ge(m_s2, C["s2"])
                if jt > 1:
                    v.wait_ge(m_st2, C["st2"])
                i = v.reduce_max(st[:, sp : sp + 1], wb[:], axis=AX.X,
                                 apply_absolute_value=True)
                i.then_inc(m_v2, 1)
                C["v2"] += 1
                v.wait_ge(m_v2, C["v2"])
                i = v.reciprocal(st[:, sp + 1 : sp + 2], st[:, sp : sp + 1])
                i.then_inc(m_v2, 1)
                C["v2"] += 1
                v.wait_ge(m_v2, C["v2"])
                v.tensor_scalar_mul(st[:, sp + 2 : sp + 3], st[:, sp + 1 : sp + 2], 127.0)
                i = v.tensor_scalar_mul(st[:, sp + 3 : sp + 4], st[:, sp : sp + 1], 1.0 / 127.0)
                i.then_inc(m_v2, 1)
                C["v2"] += 1
            @blk.scalar
            def _(s, jt=jt, wb=wb, sp=sp):
                s.wait_ge(m_v2, C["v2"])
                i = s.activation(wsrep[:], wb[:], AF.Copy, scale=st[:, sp + 2 : sp + 3])
                i.then_inc(m_s2, 1)
                C["s2"] += 1
                pw_s_snap[jt] = C["s2"]
            @blk.vector
            def _(v):
                v.wait_ge(m_s2, C["s2"])
                v.wait_ge(m_st2, C["st2"])
                def vf2(i):
                    i.then_inc(m_v2, 1)
                    C["v2"] += 1
                vf2(v.tensor_copy(iq[:], wsrep[:, 0 : D // 2]))
                v.wait_ge(m_v2, C["v2"])
                vf2(v.tensor_copy(slab[:, o_pwa : o_pwa + D // 2], iq[:]))
                v.wait_ge(m_v2, C["v2"])
                vf2(v.tensor_copy(iq[:], wsrep[:, D // 2 : D]))
                v.wait_ge(m_v2, C["v2"])
                vf2(v.tensor_copy(slab[:, o_pwa + D // 2 : o_pwa + D], iq[:]))
            for g in range(max(1, nD // 8)):
                GK = min(8, nD)
                @blk.tensor
                def _(t, g=g, GK=GK):
                    t.wait_ge(m_v2, C["v2"])
                    for k in range(GK):
                        dt = g * GK + k
                        i = t.matmul(ps_tr[:, 1024 + k * 128 : 1024 + (k + 1) * 128],
                                     slab[:, o_pwa + dt * 128 : o_pwa + (dt + 1) * 128],
                                     ident[:], is_transpose=True, start=True, stop=True)
                    inc(i, "t2")
                @blk.vector
                def _(v, g=g, GK=GK):
                    v.wait_ge(m_t2, C["t2"])
                    dst = bass.AP(slab, o_pwb + (GK * g) * 128,
                                  [[SLAB, 128], [128, GK], [1, 128]])
                    i = v.tensor_copy(dst, ps_tr[:, 1024 : 1024 + GK * 128])
                    i.then_inc(m_v2, 1)
                    C["v2"] += 1
            @blk.sync
            def _(sy):
                sy.wait_ge(m_v2, C["v2"])
                i = sy.dma_start(
                    out=bass.AP(wq_hbm.tensor, jt * 128,
                                [[nJ * 128, 128], [128 * nJ * 128, nD], [1, 128]]),
                    in_=bass.AP(slab, o_pwb, [[SLAB, 128], [128, nD], [1, 128]]))
                inc(i, "st2", 16)
                i = sy.dma_start(out=bass.AP(wsc_hbm.tensor, jt * 128, [[1, 128], [1, 1]]),
                                 in_=st[:, sp + 3 : sp + 4])
                inc(i, "st2", 16)

        xrow_b = slab[:, 0:D]
        pa1_v_snap = {}

        def emit_pa1(tt):
            xb = fa if tt % 2 == 0 else fd
            @blk.sync
            def _(sy, tt=tt, xb=xb):
                if tt == 0:
                    sy.wait_ge(m_v, C["v"])
                elif tt >= 2:
                    sy.wait_ge(m_v, pa1_v_snap[tt - 2])
                xck = D // 8
                for ck in range(8):
                    i = sy.dma_start(out=xb[:, ck * xck : (ck + 1) * xck],
                                     in_=x_all[tt * 128 : (tt + 1) * 128,
                                               ck * xck : (ck + 1) * xck])
                    inc(i, "ld", 16)
            @blk.vector
            def _(v, tt=tt, xb=xb):
                v.wait_ge(m_ld, C["ld"])
                v.wait_ge(m_st, C["st"])
                vfin(v.tensor_copy(xrow_b, xb[:]))
                pa1_v_snap[tt] = C["v"]
            for g in range(max(1, nD // 8)):
                GK = min(8, nD)
                @blk.tensor
                def _(t, g=g, GK=GK):
                    t.wait_ge(m_v, C["v"])
                    for k in range(GK):
                        dt = g * GK + k
                        i = t.matmul(ps_tr[:, k * 128 : (k + 1) * 128],
                                     slab[:, dt * 128 : (dt + 1) * 128], ident[:],
                                     is_transpose=True, start=True, stop=True)
                    inc(i, "t")
                @blk.vector
                def _(v, g=g, GK=GK):
                    v.wait_ge(m_t, C["t"])
                    dst = bass.AP(slab, D + (GK * g) * 128,
                                  [[SLAB, 128], [128, GK], [1, 128]])
                    vfin(v.tensor_copy(dst, ps_tr[:, 0 : GK * 128]))
            @blk.sync
            def _(sy):
                sy.wait_ge(m_v, C["v"])
                i = sy.dma_start(
                    out=bass.AP(xT_hbm.tensor, tt * 128,
                                [[T, 128], [128 * T, nD], [1, 128]]),
                    in_=bass.AP(slab, D, [[SLAB, 128], [128, nD], [1, 128]]))
                inc(i, "st", 16)

        for i in range(nTc):
            emit_pa1(i)
            for jt in range(i * nJ // nTc, (i + 1) * nJ // nTc):
                emit_pw(jt)

        ld_snap = {}
        for sweep in range(2):
            for dt in range(nD):
                o_xt = (dt % 2) * T
                @blk.sync
                def _(sy, sweep=sweep, dt=dt, o_xt=o_xt):
                    if sweep == 0 and dt == 0:
                        sy.wait_ge(m_st, C["st"])
                        sy.wait_ge(m_v, C["v"])
                    k = (sweep, dt)
                    kprev = (sweep, dt - 2) if dt >= 2 else ((sweep - 1, nD - 2 + dt) if sweep > 0 else None)
                    if kprev is not None and kprev in ld_snap:
                        sy.wait_ge(m_t, ld_snap[kprev])
                    i = sy.dma_start(
                        out=slab[:, o_xt : o_xt + T],
                        in_=bass.AP(xT_hbm.tensor, dt * 128 * T, [[T, 128], [1, T]]))
                    inc(i, "l%d" % (dt % 2), 16)
                @blk.tensor
                def _(t, sweep=sweep, dt=dt, o_xt=o_xt):
                    t.wait_ge(m_l[dt % 2], C["l%d" % (dt % 2)])
                    if sweep == 1 and dt == 0:
                        t.wait_ge(m_v, C["v"])
                    for tc in range(T // 512):
                        if sweep == 0:
                            i = t.matmul(ps_big[:, tc * 512 : (tc + 1) * 512],
                                         xpwT[:, dt * XP : dt * XP + 128],
                                         slab[:, o_xt + tc * 512 : o_xt + (tc + 1) * 512],
                                         start=(dt == 0), stop=(dt == nD - 1))
                        else:
                            i = t.matmul(ps_big[0:32, tc * 512 : (tc + 1) * 512],
                                         xpwT[:, dt * XP + 128 : dt * XP + XP],
                                         slab[:, o_xt + tc * 512 : o_xt + (tc + 1) * 512],
                                         start=(dt == 0), stop=(dt == nD - 1))
                    inc(i, "t")
                    ld_snap[(sweep, dt)] = C["t"]
            if sweep == 0:
                @blk.vector
                def _(v):
                    v.wait_ge(m_t, C["t"])
                    v.wait_ge(m_s2, C["s2"])
                    v.tensor_copy(fb[:, 0:T], ps_big[:, 0:T])
                    vdone(v)
            else:
                @blk.vector
                def _(v):
                    v.wait_ge(m_t, C["t"])
                    v.wait_ge(m_s2, C["s2"])
                    v.tensor_copy(fc[0:32, 0:T], ps_big[0:32, 0:T])
                    vdone(v)
        @blk.vector
        def _(v):
            v.tensor_copy(slab[:, 2 * T : 2 * T + T], fb[:, 0:T])
            v.tensor_copy(slab[0:32, o_bc : o_bc + T], fc[0:32, 0:T])
            vdone(v)
        @blk.sync
        def _(sy):
            sy.wait_ge(m_v, C["v"])
            i = sy.dma_start(out=bc_hbm[:], in_=slab[0:32, o_bc : o_bc + T])
            inc(i, "st", 16)

        o_xpb = 2 * T
        s_snap = {}
        v_snap = {}
        for dt in range(nD):
            o_xt = (dt % 2) * T
            @blk.sync
            def _(sy, dt=dt, o_xt=o_xt):
                if dt < 2:
                    sy.wait_ge(m_t, C["t"])
                if dt >= 2:
                    sy.wait_ge(m_v, v_snap[dt - 2])
                i = sy.dma_start(
                    out=slab[:, o_xt : o_xt + T],
                    in_=bass.AP(xT_hbm.tensor, dt * 128 * T, [[T, 128], [1, T]]))
                inc(i, "l%d" % (dt % 2), 16)
            @blk.tensor
            def _(t, dt=dt):
                if dt == 0:
                    t.wait_ge(m_v, C["v"])
                if dt >= 1:
                    t.wait_ge(m_s, s_snap[dt - 1])
                for tc in range(T // 512):
                    i = t.matmul(ps_big[:, tc * 512 : (tc + 1) * 512],
                                 slab[:, o_dtwT + dt * 128 : o_dtwT + (dt + 1) * 128],
                                 slab[:, o_xpb + tc * 512 : o_xpb + (tc + 1) * 512],
                                 start=True, stop=True)
                inc(i, "t")
            @blk.scalar
            def _(s, dt=dt):
                s.wait_ge(m_t, C["t"])
                if dt >= 1:
                    s.wait_ge(m_st, C["st"])
                    s.wait_ge(m_v, C["v"])
                if dt >= 1:
                    s.wait_ge(m_s, C["s"])
                sfin(s.activation(fa[:, 0:T], ps_big[:, 0:T], AF.Exp,
                                  bias=dtb_sb[:, dt : dt + 1]))
                s.wait_ge(m_s, C["s"])
                sfin(s.activation(slab[:, 2 * T + T : 2 * T + T + T], fa[:, 0:T],
                                  AF.Ln, bias=1.0))
                s_snap[dt] = C["s"]
            @blk.vector
            def _(v, dt=dt, o_xt=o_xt):
                v.wait_ge(m_l[dt % 2], C["l%d" % (dt % 2)])
                v.wait_ge(m_s, C["s"])
                if dt >= 1:
                    v.wait_ge(m_st, C["st"])
                o_dtt = 2 * T + T
                o_ut = 2 * T + 2 * T
                vfin(v.tensor_tensor(slab[:, o_ut : o_ut + T], slab[:, o_dtt : o_dtt + T],
                                     slab[:, o_xt : o_xt + T], OP.mult))
                vfin(v.tensor_scalar(slab[:, o_ut + T : o_ut + T + L],
                                     slab[:, o_xt + P : o_xt + T],
                                     Dv_sb[:, dt : dt + 1], None, OP.mult))
                v_snap[dt] = C["v"]
            @blk.sync
            def _(sy, dt=dt):
                sy.wait_ge(m_v, C["v"])
                o_dtt = 2 * T + T
                o_ut = 2 * T + 2 * T
                i = sy.dma_start(out=bass.AP(dt_hbm.tensor, dt * 128 * T, [[T, 128], [1, T]]),
                                 in_=slab[:, o_dtt : o_dtt + T])
                inc(i, "st", 16)
                i = sy.dma_start(out=bass.AP(u_hbm.tensor, dt * 128 * T, [[T, 128], [1, T]]),
                                 in_=slab[:, o_ut : o_ut + T])
                inc(i, "st", 16)
                i = sy.dma_start(out=bass.AP(gd_hbm.tensor, dt * 128 * L, [[L, 128], [1, L]]),
                                 in_=slab[:, o_ut + T : o_ut + T + L])
                inc(i, "st", 16)

        NQ = 4
        o_br = 0
        o_cr = o_br + N * P
        o_dA = o_cr + N * L
        o_dbx = o_dA + 8 * P
        o_h = o_dbx + 2 * P
        o_g = o_h + 2 * P
        o_dtu = o_g + 4 * L
        o_gd = o_dtu + 4 * P
        o_ones = o_gd + L
        assert o_ones + P <= SLAB

        ps_v_snap = {}
        ps_s_snap = {}
        ps_evac_snap = {}
        @blk.vector
        def _(v):
            v.wait_ge(m_st, C["st"])
            v.wait_ge(m_st2, C["st2"])
            vfin(v.memset(slab[:, o_ones : o_ones + P], 1.0))
        ps_v0 = C["v"]

        for ch in range(2):
            t0 = ch * P
            @blk.sync
            def _(sy, ch=ch, t0=t0):
                sy.wait_ge(m_v, C["v"])
                sy.wait_ge(m_v2, C["v2"])
                sy.wait_ge(m_st, C["st"])
                sy.wait_ge(m_st2, C["st2"])
                if DBG:
                    sy.wait_ge(m_o, C["o"])
                for n in range(N):
                    i = sy.dma_start(
                        out=slab[:, o_br + n * P : o_br + (n + 1) * P],
                        in_=bass.AP(bc_hbm.tensor, n * T + t0, [[0, 128], [1, P]]))
                    inc(i, "ld", 16)
                if ch == 1:
                    for n in range(N):
                        i = sy.dma_start(
                            out=slab[:, o_cr + n * L : o_cr + (n + 1) * L],
                            in_=bass.AP(bc_hbm.tensor, (N + n) * T + P, [[0, 128], [1, L]]))
                        inc(i, "ld", 16)

            for dt in range(nD):
                pyt = py0
                o_dt_ = o_dtu + (dt % 2) * P
                o_ut_ = o_dtu + 2 * P + (dt % 2) * P
                @blk.sync
                def _(sy, ch=ch, dt=dt, t0=t0, o_dt_=o_dt_, o_ut_=o_ut_):
                    key = (ch, dt - 2)
                    if key in ps_s_snap:
                        sy.wait_ge(m_s, ps_s_snap[key])
                        sy.wait_ge(m_v, ps_v_snap[key])
                    elif ch == 1 and dt < 2:
                        sy.wait_ge(m_s, ps_s_snap[(0, nD - 2 + dt)])
                        sy.wait_ge(m_v, ps_v_snap[(0, nD - 2 + dt)])
                    i = sy.dma_start(out=slab[:, o_dt_ : o_dt_ + P],
                                     in_=bass.AP(dt_hbm.tensor, dt * 128 * T + t0,
                                                 [[T, 128], [1, P]]))
                    inc(i, "l%d" % (dt % 2), 16)
                    i = sy.dma_start(out=slab[:, o_ut_ : o_ut_ + P],
                                     in_=bass.AP(u_hbm.tensor, dt * 128 * T + t0,
                                                 [[T, 128], [1, P]]))
                    inc(i, "l%d" % (dt % 2), 16)
                    if ch == 1:
                        if dt >= 1:
                            sy.wait_ge(m_v, ps_evac_snap[dt - 1])
                            sy.wait_ge(m_st, C["st"])
                        i = sy.dma_start(out=slab[:, o_gd : o_gd + L],
                                         in_=bass.AP(gd_hbm.tensor, dt * 128 * L,
                                                     [[L, 128], [1, L]]))
                        inc(i, "l%d" % (dt % 2), 16)

                if ch == 0:
                    @blk.vector
                    def _(v, dt=dt, o_dt_=o_dt_):
                        v.wait_ge(m_l[dt % 2], C["l%d" % (dt % 2)])
                        if dt >= 1:
                            v.wait_ge(m_s, C["s"])
                        vfin(nc.vector.tensor_tensor_scan(
                            fc[:, 0:P], slab[:, o_ones : o_ones + P],
                            slab[:, o_dt_ : o_dt_ + P], 0.0, OP.mult, OP.subtract))
                        v.wait_ge(m_v, C["v"])
                        vfin(v.tensor_scalar(bias16[:], A_sb[:, dt * N : (dt + 1) * N],
                                             fc[:, P - 1 : P], -1.0, OP.mult, OP.mult))
                else:
                    @blk.tensor
                    def _(t, dt=dt, pyt=pyt):
                        t.wait_ge(m_l[dt % 2], C["l%d" % (dt % 2)])
                        if dt >= 1:
                            t.wait_ge(m_v, ps_evac_snap[dt - 1])
                        for q in range(L // QL):
                            i = t.matmul(pyt[:, q * QL : (q + 1) * QL], ident[:],
                                         slab[:, o_gd + q * QL : o_gd + (q + 1) * QL],
                                         start=True, stop=False)
                        inc(i, "t")

                for q0 in range(0, N, NQ):
                    qb = o_dA + ((q0 // NQ) % 2) * NQ * P
                    if ch == 0:
                        @blk.scalar
                        def _(s, dt=dt, q0=q0, qb=qb):
                            if q0 == 0:
                                s.wait_ge(m_v, C["v"])
                            par = (q0 // NQ) % 2
                            for k in range(NQ):
                                n = q0 + k
                                wv = C.get(("dA", par, k), ps_v0)
                                s.wait_ge(m_v, wv)
                                sfin(s.activation(slab[:, qb + k * P : qb + (k + 1) * P],
                                                  fc[:, 0:P], AF.Exp,
                                                  scale=A_sb[:, dt * N + n : dt * N + n + 1],
                                                  bias=bias16[:, n : n + 1]))
                        @blk.vector
                        def _(v, dt=dt, q0=q0, qb=qb, o_ut_=o_ut_):
                            s_last = C["s"]
                            if q0 == 0:
                                v.wait_ge(m_l[dt % 2], C["l%d" % (dt % 2)])
                                v.wait_ge(m_ld, C["ld"])
                            ack = {}
                            def emit(tag, wait_tags, fn):
                                for w in wait_tags:
                                    wv = ack.get(w) if isinstance(w, str) else w
                                    if wv:
                                        v.wait_ge(m_v, wv)
                                i = fn()
                                i.then_inc(m_v, 1)
                                C["v"] += 1
                                ack[tag] = C["v"]
                            def B(k):
                                n = q0 + k
                                pb_ = o_dbx + (k % 2) * P
                                return v.tensor_tensor(slab[:, pb_ : pb_ + P],
                                                       slab[:, o_ut_ : o_ut_ + P],
                                                       slab[:, o_br + n * P : o_br + (n + 1) * P],
                                                       OP.mult)
                            def ST(k):
                                n = q0 + k
                                pb_ = o_dbx + (k % 2) * P
                                ph_ = o_h + (k % 2) * P
                                return nc.vector.scalar_tensor_tensor(
                                    slab[:, ph_ : ph_ + P], slab[:, pb_ : pb_ + P],
                                    1.0, slab[:, qb + k * P : qb + (k + 1) * P],
                                    OP.mult, OP.mult,
                                    accum_out=carry[:, dt * N + n : dt * N + n + 1])
                            def STW(k):
                                v.wait_ge(m_s, s_last - (NQ - 1 - k))
                                return ST(k)
                            emit("B0", [C.get("pbx0", 0)], lambda: B(0))
                            emit("B1", [C.get("pbx1", 0)], lambda: B(1))
                            emit("S0", ["B0"], lambda: STW(0))
                            emit("S1", ["B1"], lambda: STW(1))
                            emit("B2", ["S0"], lambda: B(2))
                            emit("B3", ["S1"], lambda: B(3))
                            emit("S2", ["B2", "S0"], lambda: STW(2))
                            emit("S3", ["B3", "S1"], lambda: STW(3))
                            C["pbx0"] = ack["S2"]
                            C["pbx1"] = ack["S3"]
                            par = (q0 // NQ) % 2
                            for k in range(NQ):
                                C[("dA", par, k)] = ack["S%d" % k]
                    else:
                        @blk.scalar
                        def _(s, dt=dt, q0=q0, qb=qb, o_dt_=o_dt_):
                            if q0 == 0:
                                s.wait_ge(m_l[dt % 2], C["l%d" % (dt % 2)])
                                s.wait_ge(m_ld, C["ld"])
                            par = (q0 // NQ) % 2
                            for k in range(NQ):
                                n = q0 + k
                                wv = C.get(("dA", par, k), ps_v0)
                                s.wait_ge(m_v, wv)
                                sfin(s.activation(slab[:, qb + k * P : qb + (k + 1) * P],
                                                  slab[:, o_dt_ : o_dt_ + P], AF.Exp,
                                                  scale=A_sb[:, dt * N + n : dt * N + n + 1]))
                        @blk.vector
                        def _(v, dt=dt, q0=q0, qb=qb, o_ut_=o_ut_):
                            s_last = C["s"]
                            if q0 == 0:
                                v.wait_ge(m_l[dt % 2], C["l%d" % (dt % 2)])
                                v.wait_ge(m_ld, C["ld"])
                            ack = {}
                            def emit(tag, wait_tags, fn):
                                for w in wait_tags:
                                    wv = ack.get(w) if isinstance(w, str) else w
                                    if wv:
                                        v.wait_ge(m_v, wv)
                                i = fn()
                                i.then_inc(m_v, 1)
                                C["v"] += 1
                                ack[tag] = C["v"]
                            def B(k):
                                n = q0 + k
                                pb_ = o_dbx + (k % 2) * P
                                return v.tensor_tensor(slab[:, pb_ : pb_ + P],
                                                       slab[:, o_ut_ : o_ut_ + P],
                                                       slab[:, o_br + n * P : o_br + (n + 1) * P],
                                                       OP.mult)
                            def TT_(k):
                                n = q0 + k
                                pb_ = o_dbx + (k % 2) * P
                                ph_ = o_h + (k % 2) * P
                                return nc.vector.tensor_tensor_scan(
                                    slab[:, ph_ : ph_ + P], slab[:, qb + k * P : qb + (k + 1) * P],
                                    slab[:, pb_ : pb_ + P],
                                    carry[:, dt * N + n : dt * N + n + 1],
                                    OP.mult, OP.add)
                            def CM(k):
                                n = q0 + k
                                ph_ = o_h + (k % 2) * P
                                pg = o_g + k * L
                                return v.tensor_tensor(slab[:, pg : pg + L],
                                                       slab[:, ph_ : ph_ + P],
                                                       slab[:, o_cr + n * L : o_cr + (n + 1) * L],
                                                       OP.mult)
                            def TW(k):
                                v.wait_ge(m_s, s_last - (NQ - 1 - k))
                                return TT_(k)
                            def CW(k):
                                wv = C.get(("gt", k), 0)
                                if wv:
                                    v.wait_ge(m_t, wv)
                                return CM(k)
                            emit("B0", [C.get("pbx0", 0)], lambda: B(0))
                            emit("B1", [C.get("pbx1", 0)], lambda: B(1))
                            emit("T0", ["B0"], lambda: TW(0))
                            emit("T1", ["B1"], lambda: TW(1))
                            emit("C0", ["T0"], lambda: CW(0))
                            emit("B2", ["T0"], lambda: B(2))
                            emit("C1", ["T1"], lambda: CW(1))
                            emit("B3", ["T1"], lambda: B(3))
                            emit("T2", ["B2", "C0"], lambda: TW(2))
                            emit("T3", ["B3", "C1"], lambda: TW(3))
                            emit("C2", ["T2"], lambda: CW(2))
                            emit("C3", ["T3"], lambda: CW(3))
                            C["pbx0"] = ack["T2"]
                            C["pbx1"] = ack["T3"]
                            par = (q0 // NQ) % 2
                            for k in range(NQ):
                                C[("dA", par, k)] = ack["T%d" % k]
                                C[("g", k)] = ack["C%d" % k]
                        @blk.tensor
                        def _(t, dt=dt, q0=q0):
                            for k in range(NQ):
                                n = q0 + k
                                pg = o_g + k * L
                                t.wait_ge(m_v, C[("g", k)])
                                for qq in range(L // QL):
                                    i = t.matmul(py0[:, qq * QL : (qq + 1) * QL], ident[:],
                                                 slab[:, pg + qq * QL : pg + (qq + 1) * QL],
                                                 start=False, stop=(n == N - 1))
                                inc(i, "t")
                                C[("gt", k)] = C["t"]
                ps_s_snap[(ch, dt)] = C["s"]
                ps_v_snap[(ch, dt)] = C["v"]
                if ch == 1:
                    @blk.vector
                    def _(v, dt=dt, pyt=pyt):
                        v.wait_ge(m_t, C["t"])
                        vfin(v.tensor_copy(slab[:, o_gd : o_gd + L], pyt[:, 0:L]))
                        ps_evac_snap[dt] = C["v"]
                    @blk.sync
                    def _(sy, dt=dt):
                        sy.wait_ge(m_v, C["v"])
                        i = sy.dma_start(
                            out=bass.AP(y_hbm.tensor, dt * 128 * L, [[L, 128], [1, L]]),
                            in_=slab[:, o_gd : o_gd + L])
                        inc(i, "st", 16)

        @blk.sync
        def _(sy):
            sy.wait_ge(m_st, C["st"])
            sy.wait_ge(m_st2, C["st2"])
            i = sy.dma_start(out=wsrep[:], in_=bass.AP(wsc_hbm.tensor, 0, [[0, 128], [1, D]]))
            inc(i, "ld", 16)

        o_ysl = 0
        GT0 = min(8, nD)
        o_xq = 2 * GT0 * 128
        o_xqT = o_xq + D
        o_wg = o_xqT + nD * 128 * TTH
        wg_t_snap = {}
        assert o_wg + 2 * nD * JG <= SLAB
        ps_ffn = [ps_big[:, 0:512], ps_big[:, 512:1024],
                  ps_big[:, 1024:1536], ps_big[:, 1536:2048]]

        for half in range(nHALF):
            tts = list(range(half * TTH, (half + 1) * TTH))
            yt_snap = {}
            yld_snap = {}
            for tt in tts:
                @blk.sync
                def _(sy, tt=tt):
                    sy.wait_ge(m_v, C["v"])
                    sy.wait_ge(m_s, C["s"])
                    i = sy.dma_start(out=fa[:], in_=x_all[P + tt * 128 : P + (tt + 1) * 128, :])
                    inc(i, "ld", 16)
                GT = min(8, nD)
                nG = nD // GT
                for g in range(nG):
                    ob = o_ysl + (g % 2) * GT * 128
                    @blk.sync
                    def _(sy, tt=tt, g=g, GT=GT, ob=ob):
                        if (tt, g - 2) in yt_snap:
                            sy.wait_ge(m_t, yt_snap[(tt, g - 2)])
                        dt0 = g * GT
                        i = sy.dma_start(
                            out=bass.AP(slab, ob, [[SLAB, 128], [128, GT], [1, 128]]),
                            in_=bass.AP(y_hbm.tensor, dt0 * 128 * L + tt * 128,
                                        [[L, 128], [128 * L, GT], [1, 128]]))
                        inc(i, "l%d" % (g % 2), 16)
                        yld_snap[(tt, g)] = C["l%d" % (g % 2)]
                    @blk.tensor
                    def _(t, tt=tt, g=g, GT=GT, ob=ob):
                        t.wait_ge(m_l[g % 2], yld_snap[(tt, g)])
                        if g > 0:
                            t.wait_ge(m_v, C["v"])
                        for k in range(GT):
                            i = t.matmul(ps_tr[:, k * 128 : (k + 1) * 128],
                                         slab[:, ob + k * 128 : ob + (k + 1) * 128],
                                         ident[:], is_transpose=True, start=True, stop=True)
                        inc(i, "t")
                        yt_snap[(tt, g)] = C["t"]
                    @blk.vector
                    def _(v, tt=tt, g=g, GT=GT):
                        v.wait_ge(m_t, C["t"])
                        if g == 0:
                            v.wait_ge(m_ld, C["ld"])
                            v.wait_ge(m_st, C["st"])
                            v.wait_ge(m_o, C["o"])
                        vfin(v.tensor_tensor(fb[:, g * GT * 128 : (g + 1) * GT * 128],
                                             ps_tr[:, 0 : GT * 128],
                                             fa[:, g * GT * 128 : (g + 1) * GT * 128],
                                             OP.add))
                @blk.vector
                def _(v, tt=tt):
                    v.wait_ge(m_v, C["v"])
                    vfin(v.reduce_sum(st[:, 4:5], fb[:], axis=AX.X))
                    v.wait_ge(m_v, C["v"])
                    vfin(v.tensor_scalar_mul(st[:, 5:6], st[:, 4:5], 1.0 / D))
                @blk.scalar
                def _(s, tt=tt):
                    s.wait_ge(m_v, C["v"])
                    sfin(s.activation(fc[:], fb[:], AF.Square, accum_out=st[:, 6:7]))
                @blk.vector
                def _(v, tt=tt):
                    v.wait_ge(m_s, C["s"])
                    vfin(v.tensor_scalar_mul(st[:, 7:8], st[:, 6:7], 1.0 / D))
                    vfin(v.tensor_tensor(st[:, 8:9], st[:, 5:6], st[:, 5:6], OP.mult))
                    v.wait_ge(m_v, C["v"])
                    vfin(v.tensor_tensor(st[:, 9:10], st[:, 7:8], st[:, 8:9], OP.subtract))
                    v.wait_ge(m_v, C["v"])
                    vfin(v.tensor_scalar_add(st[:, 9:10], st[:, 9:10], 1e-5))
                @blk.scalar
                def _(s, tt=tt):
                    s.wait_ge(m_v, C["v"])
                    sfin(s.activation(st[:, 10:11], st[:, 9:10], AF.Ln))
                    s.wait_ge(m_s, C["s"])
                    sfin(s.activation(st[:, 11:12], st[:, 10:11], AF.Exp, scale=-0.5))
                @blk.vector
                def _(v, tt=tt):
                    v.wait_ge(m_s, C["s"])
                    vfin(v.tensor_scalar(fb[:], fb[:], st[:, 5:6], st[:, 11:12],
                                         OP.subtract, OP.mult))
                @blk.sync
                def _(sy, tt=tt):
                    sy.wait_ge(m_v, C["v"])
                    i = sy.dma_start(out=bass.AP(xm_hbm.tensor, tt * 128 * D, [[D, 128], [1, D]]),
                                     in_=fb[:])
                    inc(i, "st", 16)
                @blk.vector
                def _(v, tt=tt):
                    v.wait_ge(m_v, C["v"])
                    vfin(v.reduce_max(st[:, 14:15], fb[:], axis=AX.X,
                                      apply_absolute_value=True))
                    v.wait_ge(m_v, C["v"])
                    vfin(v.reciprocal(st[:, 15:16], st[:, 14:15]))
                    v.wait_ge(m_v, C["v"])
                    vfin(v.tensor_scalar_mul(qsc[:, tt : tt + 1], st[:, 15:16], 127.0))
                    vfin(v.tensor_scalar_mul(asc[:, tt : tt + 1], st[:, 14:15], 1.0 / 127.0))
                @blk.scalar
                def _(s, tt=tt):
                    s.wait_ge(m_v, C["v"])
                    sfin(s.activation(fc[:], fb[:], AF.Copy, scale=qsc[:, tt : tt + 1]))
                @blk.vector
                def _(v, tt=tt):
                    v.wait_ge(m_s, C["s"])
                    vfin(v.tensor_copy(iq[:], fc[:, 0 : D // 2]))
                    v.wait_ge(m_v, C["v"])
                    vfin(v.tensor_copy(slab[:, o_xq : o_xq + D // 2], iq[:]))
                    v.wait_ge(m_v, C["v"])
                    vfin(v.tensor_copy(iq[:], fc[:, D // 2 : D]))
                    v.wait_ge(m_v, C["v"])
                    vfin(v.tensor_copy(slab[:, o_xq + D // 2 : o_xq + D], iq[:]))
                for g in range(nD // GT):
                    @blk.tensor
                    def _(t, tt=tt, g=g, GT=GT):
                        t.wait_ge(m_v, C["v"])
                        for k in range(GT):
                            dt = g * GT + k
                            i = t.matmul(ps_tr[:, k * 128 : (k + 1) * 128],
                                         slab[:, o_xq + dt * 128 : o_xq + (dt + 1) * 128],
                                         ident[:], is_transpose=True, start=True, stop=True)
                        inc(i, "t")
                    @blk.vector
                    def _(v, tt=tt, g=g, GT=GT):
                        v.wait_ge(m_t, C["t"])
                        toff = tt - half * TTH
                        dst = bass.AP(slab,
                                      o_xqT + (GT * g) * (128 * TTH) + toff * 128,
                                      [[SLAB, 128], [128 * TTH, GT], [1, 128]])
                        vfin(v.tensor_copy(dst, ps_tr[:, 0 : GT * 128]))
            for jg in range(nJG):
                o_wgb = o_wg + (jg % 2) * nD * JG
                @blk.sync
                def _(sy, jg=jg, o_wgb=o_wgb):
                    if jg >= 2:
                        sy.wait_ge(m_t, wg_t_snap[jg - 2])
                    jt0 = jg * JT_PER_G
                    DCH = min(8, nD)
                    for d0 in range(0, nD, DCH):
                        i = sy.dma_start(
                            out=bass.AP(slab, o_wgb + d0 * JG,
                                        [[SLAB, 128], [JG, DCH], [1, JG]]),
                            in_=bass.AP(wq_hbm.tensor,
                                        d0 * 128 * nJ * 128 + jt0 * 128,
                                        [[nJ * 128, 128], [128 * nJ * 128, DCH],
                                         [1, JG]]))
                        inc(i, "l%d" % (jg % 2), 16)
                for tt in tts:
                    toff = tt - half * TTH
                    @blk.tensor
                    def _(t, jg=jg, tt=tt, toff=toff, o_wgb=o_wgb):
                        t.wait_ge(m_l[jg % 2], C["l%d" % (jg % 2)])
                        t.wait_ge(m_v, C["v"])
                        for dt in range(nD):
                            i = t.matmul(ps_ffn[toff],
                                         slab[:, o_xqT + dt * (128 * TTH) + toff * 128
                                              : o_xqT + dt * (128 * TTH) + (toff + 1) * 128],
                                         slab[:, o_wgb + dt * JG : o_wgb + (dt + 1) * JG],
                                         start=(dt == 0), stop=(dt == nD - 1))
                        inc(i, "t")
                wg_t_snap[jg] = C["t"]
                GB = max(1, min(len(tts), D // 512))
                for b0 in range(0, len(tts), GB):
                    bts = list(range(b0, min(b0 + GB, len(tts))))
                    @blk.vector
                    def _(v, jg=jg, bts=bts):
                        v.wait_ge(m_t, C["t"])
                        v.wait_ge(m_s, C["s"])
                        for j, toff in enumerate(bts):
                            i = v.tensor_tensor(fc[:, j * 512 : (j + 1) * 512], ps_ffn[toff],
                                                wsrep[:, jg * 512 : (jg + 1) * 512], OP.mult)
                        vfin(i)
                    @blk.scalar
                    def _(s, jg=jg, bts=bts):
                        s.wait_ge(m_v, C["v"])
                        s.wait_ge(m_st, C["st"])
                        for j, toff in enumerate(bts):
                            i = s.activation(slab16[:, j * 512 : (j + 1) * 512],
                                             fc[:, j * 512 : (j + 1) * 512], AF.Copy,
                                             scale=asc[:, tts[toff] : tts[toff] + 1])
                        sfin(i)
                    @blk.sync
                    def _(sy, jg=jg, bts=bts):
                        sy.wait_ge(m_s, C["s"])
                        for j, toff in enumerate(bts):
                            i = sy.dma_start(
                                out=bass.AP(f16_hbm.tensor, tts[toff] * 128 * D + jg * 512,
                                            [[D, 128], [1, 512]]),
                                in_=slab16[:, j * 512 : (j + 1) * 512])
                            inc(i, "st", 16)
            sy_snap = {}
            for tt in tts:
                @blk.sync
                def _(sy, tt=tt):
                    sy.wait_ge(m_v, C["v"])
                    sy.wait_ge(m_st, C["st"])
                    i = sy.dma_start(out=fa[:], in_=bass.AP(xm_hbm.tensor, tt * 128 * D,
                                                            [[D, 128], [1, D]]))
                    inc(i, "ld", 16)
                    seg_snap = {}
                    for seg in range(D // 512):
                        fh = f16a if seg % 2 == 0 else f16b
                        if seg >= 2:
                            sy.wait_ge(m_v, C["v"] + seg - 1)
                        i = sy.dma_start(out=fh[:], in_=bass.AP(f16_hbm.tensor,
                                                                tt * 128 * D + seg * 512,
                                                                [[D, 128], [1, 512]]))
                        inc(i, "l%d" % (seg % 2), 16)
                        seg_snap[seg] = C["l%d" % (seg % 2)]
                    sy_snap[tt] = seg_snap
                @blk.vector
                def _(v, tt=tt):
                    v.wait_ge(m_ld, C["ld"])
                    v.wait_ge(m_o, C["o"])
                    for seg in range(D // 512):
                        fh = f16a if seg % 2 == 0 else f16b
                        v.wait_ge(m_l[seg % 2], sy_snap[tt][seg])
                        i = v.tensor_tensor(fb[:, seg * 512 : (seg + 1) * 512],
                                            fa[:, seg * 512 : (seg + 1) * 512], fh[:], OP.add)
                        i.then_inc(m_v, 1)
                        C["v"] += 1
                    v.wait_ge(m_v, C["v"])
                    vfin(v.reduce_sum(st[:, 4:5], fb[:], axis=AX.X))
                    v.wait_ge(m_v, C["v"])
                    vfin(v.tensor_scalar_mul(st[:, 5:6], st[:, 4:5], 1.0 / D))
                @blk.scalar
                def _(s, tt=tt):
                    s.wait_ge(m_v, C["v"])
                    sfin(s.activation(fc[:], fb[:], AF.Square, accum_out=st[:, 6:7]))
                @blk.vector
                def _(v, tt=tt):
                    v.wait_ge(m_s, C["s"])
                    vfin(v.tensor_scalar_mul(st[:, 7:8], st[:, 6:7], 1.0 / D))
                    vfin(v.tensor_tensor(st[:, 8:9], st[:, 5:6], st[:, 5:6], OP.mult))
                    v.wait_ge(m_v, C["v"])
                    vfin(v.tensor_tensor(st[:, 9:10], st[:, 7:8], st[:, 8:9], OP.subtract))
                    v.wait_ge(m_v, C["v"])
                    vfin(v.tensor_scalar_add(st[:, 9:10], st[:, 9:10], 1e-5))
                @blk.scalar
                def _(s, tt=tt):
                    s.wait_ge(m_v, C["v"])
                    sfin(s.activation(st[:, 10:11], st[:, 9:10], AF.Ln))
                    s.wait_ge(m_s, C["s"])
                    sfin(s.activation(st[:, 11:12], st[:, 10:11], AF.Exp, scale=-0.5))
                @blk.vector
                def _(v, tt=tt):
                    v.wait_ge(m_s, C["s"])
                    vfin(v.tensor_scalar(fb[:], fb[:], st[:, 5:6], st[:, 11:12],
                                         OP.subtract, OP.mult))
                @blk.sync
                def _(sy, tt=tt):
                    sy.wait_ge(m_v, C["v"])
                    i = sy.dma_start(out=out_hbm[tt * 128 : (tt + 1) * 128, :], in_=fb[:])
                    inc(i, "o", 16)

        @blk.gpsimd
        def _(g):
            g.wait_ge(m_o, C["o"])

    return nc


_NC_CACHE = {}


def kernel(L0_attn_x, L0_attn_A_log, L0_attn_D, L0_attn_x_proj_w,
           L0_attn_dt_proj_w, L0_attn_dt_proj_b, L0_ffn_weight):
    import ml_dtypes
    x = np.ascontiguousarray(np.asarray(L0_attn_x, dtype=np.float32))
    B, Tfull, Dm = x.shape
    N = np.asarray(L0_attn_A_log).shape[1]
    R = np.asarray(L0_attn_dt_proj_w).shape[1]
    P = Tfull // 2
    L = Tfull // 2

    key = (Dm, P, L, N, R)
    if key not in _NC_CACHE:
        _NC_CACHE[key] = _build(Dm, P, L, N, R)
    nc = _NC_CACHE[key]

    A_log = np.ascontiguousarray(np.asarray(L0_attn_A_log, np.float32))
    Dv = np.ascontiguousarray(np.asarray(L0_attn_D, np.float32))
    xpw = np.ascontiguousarray(np.asarray(L0_attn_x_proj_w, np.float32))
    dtw = np.ascontiguousarray(np.asarray(L0_attn_dt_proj_w, np.float32))
    dtbv = np.ascontiguousarray(np.asarray(L0_attn_dt_proj_b, np.float32))
    Wm = np.ascontiguousarray(np.asarray(L0_ffn_weight, np.float32))
    ident = np.eye(128, dtype=np.float32).astype(ml_dtypes.bfloat16)

    in_maps = []
    for c in range(8):
        b, h = c // 2, c % 2
        if h == 0:
            x_pre = np.zeros((P, Dm), np.float32)
            x_own = x[b, 0:L]
        else:
            x_pre = x[b, 0:P]
            x_own = x[b, P : P + L]
        x_allc = np.ascontiguousarray(np.concatenate([x_pre, x_own], axis=0))
        in_maps.append({
            "x_all": x_allc, "A_log": A_log, "D_vec": Dv, "xpw": xpw,
            "dtw": dtw, "dtb": dtbv, "W": Wm, "ident": ident,
        })

    res = run_bass_kernel_spmd(nc, in_maps, core_ids=list(range(8)))
    out = np.empty((B, Tfull, Dm), np.float32)
    for c in range(8):
        b, h = c // 2, c % 2
        out[b, h * L : (h + 1) * L] = res.results[c]["out"]
    return out


# version 27
# speedup vs baseline: 1.1874x; 1.1871x over previous
import os
import numpy as np
import concourse.bass as bass
import concourse.mybir as mybir
from concourse.bass_utils import run_bass_kernel_spmd

F32 = mybir.dt.float32
BF16 = mybir.dt.bfloat16
F16 = mybir.dt.float16
I32 = mybir.dt.int32
AF = mybir.ActivationFunctionType
OP = mybir.AluOpType
AX = mybir.AxisListType


def _build(D, P, L, N, R):
    assert R == 128 and D % 512 == 0 and P % 128 == 0 and P == L
    QL = 512 if L % 512 == 0 else L
    T = P + L
    nD = D // 128
    nJ = D // 128
    nTT = L // 128
    nTc = T // 128
    XP = R + 2 * N
    JG = 512
    nJG = D // JG
    JT_PER_G = JG // 128
    TTH = min(4, nTT)
    nHALF = nTT // TTH

    nc = bass.Bass("TRN2", target_bir_lowering=False)

    x_all = nc.dram_tensor("x_all", [T, D], F32, kind="ExternalInput").ap()
    A_log = nc.dram_tensor("A_log", [D, N], F32, kind="ExternalInput").ap()
    D_vec = nc.dram_tensor("D_vec", [D], F32, kind="ExternalInput").ap()
    xpw = nc.dram_tensor("xpw", [XP, D], F32, kind="ExternalInput").ap()
    dtw = nc.dram_tensor("dtw", [D, R], F32, kind="ExternalInput").ap()
    dtb = nc.dram_tensor("dtb", [D], F32, kind="ExternalInput").ap()
    W = nc.dram_tensor("W", [D, D], F32, kind="ExternalInput").ap()
    ident_in = nc.dram_tensor("ident", [128, 128], BF16, kind="ExternalInput").ap()
    out_hbm = nc.dram_tensor("out", [L, D], F32, kind="ExternalOutput").ap()

    wq_hbm = nc.dram_tensor("wq_st", [nD, 128, nJ * 128], BF16).ap()
    wsc_hbm = nc.dram_tensor("wsc_st", [D], F32).ap()
    xT_hbm = nc.dram_tensor("xT_st", [nD, 128, T], BF16).ap()
    dt_hbm = nc.dram_tensor("dt_st", [nD, 128, T], BF16).ap()
    u_hbm = nc.dram_tensor("u_st", [nD, 128, T], BF16).ap()
    gd_hbm = nc.dram_tensor("gd_st", [nD, 128, L], BF16).ap()
    bc_hbm = nc.dram_tensor("bc_st", [2 * N, T], BF16).ap()
    y_hbm = nc.dram_tensor("y_st", [nD, 128, L], BF16).ap()
    xm_hbm = nc.dram_tensor("xm_st", [nTT, 128, D], F32).ap()
    f16_hbm = nc.dram_tensor("f16_st", [nTT, 128, D], F16).ap()
    DBG = os.environ.get("K_DEBUG") == "1"
    if DBG:
        dbg_scum = nc.dram_tensor("dbg_scum", [128, T], F32, kind="ExternalOutput").ap()
        dbg_bias = nc.dram_tensor("dbg_bias", [128, 16], F32, kind="ExternalOutput").ap()
        dbg_dA = nc.dram_tensor("dbg_dA", [128, T], BF16, kind="ExternalOutput").ap()
        dbg_dbx = nc.dram_tensor("dbg_dbx", [128, T], BF16, kind="ExternalOutput").ap()
        dbg_br = nc.dram_tensor("dbg_br", [128, T], BF16, kind="ExternalOutput").ap()
        dbg_carry = nc.dram_tensor("dbg_carry", [128, 512], F32, kind="ExternalOutput").ap()

    SLAB = max(2 * N * P + 8 * P + 2 * P + 2 * P + 4 * L + 4 * P + L + P,
               3 * D + T,
               2 * T + T + 2 * T,
               512 + D + nD * 128 * TTH + nD * JG)
    SLAB = max(SLAB, 4 * D, max(5 * T + L, 3 * D) + nD * 128 + T)

    from contextlib import ExitStack
    with ExitStack() as _es:
        def _e(cm):
            return _es.enter_context(cm)
        slab = _e(nc.sbuf_tensor("slab", [128, SLAB], BF16))
        fa = _e(nc.sbuf_tensor("fa", [128, D], F32))
        fb = _e(nc.sbuf_tensor("fb", [128, D], F32))
        fc = _e(nc.sbuf_tensor("fc", [128, D], F32))
        fd = _e(nc.sbuf_tensor("fd", [128, D], F32))
        iq = _e(nc.sbuf_tensor("iq", [128, D // 2], I32))
        wsrep = _e(nc.sbuf_tensor("wsrep", [128, D], F32))
        A_sb = _e(nc.sbuf_tensor("A_sb", [128, nD * N], F32))
        dtb_sb = _e(nc.sbuf_tensor("dtb_sb", [128, nD], F32))
        Dv_sb = _e(nc.sbuf_tensor("Dv_sb", [128, nD], F32))

        ident = _e(nc.sbuf_tensor("ident_sb", [128, 128], BF16))
        
        
        carry = _e(nc.sbuf_tensor("carry", [128, nD * N], F32))
        st = _e(nc.sbuf_tensor("st", [128, 16], F32))
        asc = _e(nc.sbuf_tensor("asc", [128, nTT], F32))
        qsc = _e(nc.sbuf_tensor("qsc", [128, nTT], F32))
        bias16 = _e(nc.sbuf_tensor("bias16", [128, 16], F32))
        f16a = _e(nc.sbuf_tensor("f16a", [128, 512], F16))
        f16b = _e(nc.sbuf_tensor("f16b", [128, 512], F16))
        slab16 = _e(nc.sbuf_tensor("slab16", [128, 4 * 512], F16))
        ps_big = _e(nc.psum_tensor("ps_big", [128, 2048], F32))
        py0 = _e(nc.psum_tensor("py0", [128, 1024], F32))
        ps_tr = _e(nc.psum_tensor("ps_tr", [128, 2048], BF16))
        m_ld = _e(nc.semaphore("m_ld"))
        m_st = _e(nc.semaphore("m_st"))
        m_v = _e(nc.semaphore("m_v"))
        m_s = _e(nc.semaphore("m_s"))
        m_t = _e(nc.semaphore("m_t"))
        m_o = _e(nc.semaphore("m_o"))
        m_l0 = _e(nc.semaphore("m_l0"))
        m_l1 = _e(nc.semaphore("m_l1"))
        m_v2 = _e(nc.semaphore("m_v2"))
        m_s2 = _e(nc.semaphore("m_s2"))
        m_t2 = _e(nc.semaphore("m_t2"))
        m_st2 = _e(nc.semaphore("m_st2"))
        m_l2 = _e(nc.semaphore("m_l2"))
        blk = _e(nc.Block())
        o_dtwT = max(5 * T + L, 3 * D)
        dtwT = slab[:, o_dtwT : o_dtwT + nD * 128]
        o_bc = o_dtwT + nD * 128
        bc_sb = slab[:, o_bc : o_bc + T]
        o_xpwT = o_bc + T
        xpwT = slab[:, o_xpwT : o_xpwT + nD * XP]
        assert o_xpwT + nD * XP <= SLAB - 2 * D
        C = {"ld": 0, "st": 0, "v": 0, "s": 0, "t": 0, "o": 0, "l0": 0, "l1": 0,
             "v2": 0, "s2": 0, "t2": 0, "st2": 0, "l2": 0}
        SEM = {"ld": m_ld, "st": m_st, "v": m_v, "s": m_s, "t": m_t, "o": m_o,
               "l0": m_l0, "l1": m_l1, "v2": m_v2, "s2": m_s2, "t2": m_t2,
               "st2": m_st2, "l2": m_l2}
        m_l = [m_l0, m_l1]

        def inc(instr, key, n=1):
            instr.then_inc(SEM[key], n)
            C[key] += n

        def vdone(v):
            v.drain()
            v.sem_inc(m_v, 1)
            C["v"] += 1

        def sdone(s):
            s.drain()
            s.sem_inc(m_s, 1)
            C["s"] += 1

        def vfin(instr):
            instr.then_inc(m_v, 1)
            C["v"] += 1

        def sfin(instr):
            instr.then_inc(m_s, 1)
            C["s"] += 1

        @blk.sync
        def _(sy):
            src = bass.AP(A_log.tensor, 0, [[N, 128], [128 * N, nD], [1, N]])
            sy.dma_start(out=A_sb[:, 0 : nD * N], in_=src).then_inc(m_ld, 16)
            sy.dma_start(out=dtb_sb[:], in_=bass.AP(dtb.tensor, 0, [[1, 128], [128, nD]]), allow_slow_non_contiguous=True).then_inc(m_ld, 16)
            sy.dma_start(out=Dv_sb[:], in_=bass.AP(D_vec.tensor, 0, [[1, 128], [128, nD]]), allow_slow_non_contiguous=True).then_inc(m_ld, 16)
            sy.dma_start(out=ident[:], in_=ident_in[:]).then_inc(m_ld, 16)
            sy.dma_start(out=fb[:], in_=xpw[0:128, :]).then_inc(m_ld, 16)
            sy.dma_start(out=fc[0:32, :], in_=xpw[128:XP, :]).then_inc(m_ld, 16)
            C["ld"] += 6 * 16

        @blk.scalar
        def _(s):
            s.add_instruction(mybir.InstLoadActFuncSet(
                name=nc.get_next_instruction_name(), act_func_set_id=6))
            s.wait_ge(m_ld, C["ld"])
            s.activation(fa[:, 0 : nD * N], A_sb[:, 0 : nD * N], AF.Exp)
            sdone(s)

        @blk.vector
        def _(v):
            v.wait_ge(m_s, C["s"])
            v.tensor_scalar_mul(A_sb[:, 0 : nD * N], fa[:, 0 : nD * N], -1.0)
            v.drain()
            v.tensor_copy(slab[:, 0:D], fb[:])
            v.tensor_copy(slab[0:32, D : 2 * D], fc[0:32, :])
            vdone(v)

        @blk.sync
        def _(sy):
            for pc in range(4):
                i = sy.dma_start(
                    out=fd[:, pc * (D // 4) : (pc + 1) * (D // 4)],
                    in_=bass.AP(dtw.tensor, pc * (nD // 4) * 128 * R,
                                [[R, 128], [128 * R, nD // 4], [1, R]]))
                inc(i, "ld", 16)
        @blk.vector
        def _(v):
            v.wait_ge(m_ld, C["ld"])
            v.tensor_copy(slab[:, 2 * D : 3 * D], fd[:])
            vdone(v)
        dtwcp_v = C["v"]

        for dt in range(nD):
            @blk.tensor
            def _(t, dt=dt):
                t.wait_ge(m_v, C["v"])
                t.matmul(ps_tr[:, 0:128], slab[:, dt * 128 : (dt + 1) * 128], ident[:],
                         is_transpose=True, start=True, stop=True)
                t.matmul(ps_tr[:, 128:160], slab[0:32, D + dt * 128 : D + (dt + 1) * 128],
                         ident[0:32, 0:32], is_transpose=True, start=True, stop=True)
                i = t.matmul(ps_tr[:, 256:384], slab[:, 2 * D + dt * 128 : 2 * D + (dt + 1) * 128],
                             ident[:], is_transpose=True, start=True, stop=True)
                inc(i, "t")
            @blk.vector
            def _(v, dt=dt):
                v.wait_ge(m_t, C["t"])
                v.tensor_copy(xpwT[:, dt * XP : dt * XP + 128], ps_tr[:, 0:128])
                v.tensor_copy(xpwT[:, dt * XP + 128 : dt * XP + XP], ps_tr[:, 128:160])
                v.tensor_copy(slab[:, o_dtwT + dt * 128 : o_dtwT + (dt + 1) * 128], ps_tr[:, 256:384])
                vdone(v)

        o_pwa = SLAB - 2 * D
        o_pwb2 = 31744
        XTB = 23552
        assert o_pwb2 + 2 * D <= SLAB and XTB + 2 * D <= o_pwb2
        pw_s_snap = {}

        def emit_pw(jt):
            wb = fb if jt % 2 == 0 else fc
            sp = (jt % 2) * 8
            @blk.scalar
            def _(sy, jt=jt, wb=wb):
                if jt < 2:
                    sy.wait_ge(m_v, C["v"])
                else:
                    sy.wait_ge(m_s2, pw_s_snap[jt - 2])
                wck = D // 16
                for ck in range(16):
                    i = sy.dma_start(out=wb[:, ck * wck : (ck + 1) * wck],
                                     in_=W[jt * 128 : (jt + 1) * 128,
                                           ck * wck : (ck + 1) * wck])
                    inc(i, "l2", 16)
            @blk.vector
            def _(v, jt=jt, wb=wb, sp=sp):
                v.wait_ge(m_l2, C["l2"])
                v.wait_ge(m_s2, C["s2"])
                if jt > 1:
                    v.wait_ge(m_st2, C["st2"])
                i = v.reduce_max(st[:, sp : sp + 1], wb[:], axis=AX.X,
                                 apply_absolute_value=True)
                i.then_inc(m_v2, 1)
                C["v2"] += 1
                v.wait_ge(m_v2, C["v2"])
                i = v.reciprocal(st[:, sp + 1 : sp + 2], st[:, sp : sp + 1])
                i.then_inc(m_v2, 1)
                C["v2"] += 1
                v.wait_ge(m_v2, C["v2"])
                v.tensor_scalar_mul(st[:, sp + 2 : sp + 3], st[:, sp + 1 : sp + 2], 127.0)
                i = v.tensor_scalar_mul(st[:, sp + 3 : sp + 4], st[:, sp : sp + 1], 1.0 / 127.0)
                i.then_inc(m_v2, 1)
                C["v2"] += 1
            @blk.scalar
            def _(s, jt=jt, wb=wb, sp=sp):
                s.wait_ge(m_v2, C["v2"])
                i = s.activation(wsrep[:], wb[:], AF.Copy, scale=st[:, sp + 2 : sp + 3])
                i.then_inc(m_s2, 1)
                C["s2"] += 1
                pw_s_snap[jt] = C["s2"]
            @blk.vector
            def _(v):
                v.wait_ge(m_s2, C["s2"])
                v.wait_ge(m_st2, C["st2"])
                def vf2(i):
                    i.then_inc(m_v2, 1)
                    C["v2"] += 1
                vf2(v.tensor_copy(iq[:], wsrep[:, 0 : D // 2]))
                v.wait_ge(m_v2, C["v2"])
                vf2(v.tensor_copy(slab[:, o_pwa : o_pwa + D // 2], iq[:]))
                v.wait_ge(m_v2, C["v2"])
                vf2(v.tensor_copy(iq[:], wsrep[:, D // 2 : D]))
                v.wait_ge(m_v2, C["v2"])
                vf2(v.tensor_copy(slab[:, o_pwa + D // 2 : o_pwa + D], iq[:]))
            for g in range(max(1, nD // 8)):
                GK = min(8, nD)
                @blk.tensor
                def _(t, g=g, GK=GK):
                    t.wait_ge(m_v2, C["v2"])
                    for k in range(GK):
                        dt = g * GK + k
                        i = t.matmul(ps_tr[:, 1024 + k * 128 : 1024 + (k + 1) * 128],
                                     slab[:, o_pwa + dt * 128 : o_pwa + (dt + 1) * 128],
                                     ident[:], is_transpose=True, start=True, stop=True)
                    inc(i, "t2")
                @blk.vector
                def _(v, jt=jt, g=g, GK=GK):
                    v.wait_ge(m_t2, C["t2"])
                    dst = bass.AP(slab, o_pwb2 + (jt % 2) * 128 + (GK * g) * 256,
                                  [[SLAB, 128], [256, GK], [1, 128]])
                    i = v.tensor_copy(dst, ps_tr[:, 1024 : 1024 + GK * 128])
                    i.then_inc(m_v2, 1)
                    C["v2"] += 1
            @blk.sync
            def _(sy, jt=jt, sp=sp):
                sy.wait_ge(m_v2, C["v2"])
                if jt % 2 == 1:
                    for pc in range(8):
                        i = sy.dma_start(
                            out=bass.AP(wq_hbm.tensor,
                                        (jt - 1) * 128 + pc * 4 * 128 * nJ * 128,
                                        [[nJ * 128, 128], [128 * nJ * 128, 4], [1, 256]]),
                            in_=bass.AP(slab, o_pwb2 + pc * 4 * 256,
                                        [[SLAB, 128], [256, 4], [1, 256]]))
                        inc(i, "st2", 16)
                i = sy.dma_start(out=bass.AP(wsc_hbm.tensor, jt * 128, [[1, 128], [1, 1]]),
                                 in_=st[:, sp + 3 : sp + 4])
                inc(i, "st2", 16)

        xrow_b = slab[:, 0:D]
        pa1_v_snap = {}

        def emit_pa1(tt):
            xb = fa if tt % 2 == 0 else fd
            @blk.sync
            def _(sy, tt=tt, xb=xb):
                if tt == 0:
                    sy.wait_ge(m_v, C["v"])
                elif tt == 1:
                    sy.wait_ge(m_v, dtwcp_v)
                elif tt >= 2:
                    sy.wait_ge(m_v, pa1_v_snap[tt - 2])
                xck = D // 8
                for ck in range(8):
                    i = sy.dma_start(out=xb[:, ck * xck : (ck + 1) * xck],
                                     in_=x_all[tt * 128 : (tt + 1) * 128,
                                               ck * xck : (ck + 1) * xck])
                    inc(i, "ld", 16)
            @blk.vector
            def _(v, tt=tt, xb=xb):
                v.wait_ge(m_ld, C["ld"])
                v.wait_ge(m_st, C["st"])
                vfin(v.tensor_copy(xrow_b, xb[:]))
                pa1_v_snap[tt] = C["v"]
            for g in range(max(1, nD // 8)):
                GK = min(8, nD)
                @blk.tensor
                def _(t, g=g, GK=GK):
                    t.wait_ge(m_v, C["v"])
                    for k in range(GK):
                        dt = g * GK + k
                        i = t.matmul(ps_tr[:, k * 128 : (k + 1) * 128],
                                     slab[:, dt * 128 : (dt + 1) * 128], ident[:],
                                     is_transpose=True, start=True, stop=True)
                    inc(i, "t")
                @blk.vector
                def _(v, tt=tt, g=g, GK=GK):
                    v.wait_ge(m_t, C["t"])
                    dst = bass.AP(slab, XTB + (tt % 2) * 128 + (GK * g) * 256,
                                  [[SLAB, 128], [256, GK], [1, 128]])
                    vfin(v.tensor_copy(dst, ps_tr[:, 0 : GK * 128]))
            if tt % 2 == 1:
                @blk.sync
                def _(sy, tt=tt):
                    sy.wait_ge(m_v, C["v"])
                    for pc in range(8):
                        i = sy.dma_start(
                            out=bass.AP(xT_hbm.tensor, (tt - 1) * 128 + pc * 4 * 128 * T,
                                        [[T, 128], [128 * T, 4], [1, 256]]),
                            in_=bass.AP(slab, XTB + pc * 4 * 256,
                                        [[SLAB, 128], [256, 4], [1, 256]]))
                        inc(i, "st", 16)

        for i in range(nTc):
            emit_pa1(i)
            for jt in range(i * nJ // nTc, (i + 1) * nJ // nTc):
                emit_pw(jt)

        ld_snap = {}
        for sweep in range(2):
            for dt in range(nD):
                o_xt = (dt % 2) * T
                @blk.sync
                def _(sy, sweep=sweep, dt=dt, o_xt=o_xt):
                    if sweep == 0 and dt == 0:
                        sy.wait_ge(m_st, C["st"])
                        sy.wait_ge(m_v, C["v"])
                    k = (sweep, dt)
                    kprev = (sweep, dt - 2) if dt >= 2 else ((sweep - 1, nD - 2 + dt) if sweep > 0 else None)
                    if kprev is not None and kprev in ld_snap:
                        sy.wait_ge(m_t, ld_snap[kprev])
                    i = sy.dma_start(
                        out=slab[:, o_xt : o_xt + T],
                        in_=bass.AP(xT_hbm.tensor, dt * 128 * T, [[T, 128], [1, T]]))
                    inc(i, "l%d" % (dt % 2), 16)
                @blk.tensor
                def _(t, sweep=sweep, dt=dt, o_xt=o_xt):
                    t.wait_ge(m_l[dt % 2], C["l%d" % (dt % 2)])
                    if sweep == 1 and dt == 0:
                        t.wait_ge(m_v, C["v"])
                    for tc in range(T // 512):
                        if sweep == 0:
                            i = t.matmul(ps_big[:, tc * 512 : (tc + 1) * 512],
                                         xpwT[:, dt * XP : dt * XP + 128],
                                         slab[:, o_xt + tc * 512 : o_xt + (tc + 1) * 512],
                                         start=(dt == 0), stop=(dt == nD - 1))
                        else:
                            i = t.matmul(ps_big[0:32, tc * 512 : (tc + 1) * 512],
                                         xpwT[:, dt * XP + 128 : dt * XP + XP],
                                         slab[:, o_xt + tc * 512 : o_xt + (tc + 1) * 512],
                                         start=(dt == 0), stop=(dt == nD - 1))
                    inc(i, "t")
                    ld_snap[(sweep, dt)] = C["t"]
            if sweep == 0:
                @blk.vector
                def _(v):
                    v.wait_ge(m_t, C["t"])
                    v.wait_ge(m_s2, C["s2"])
                    v.tensor_copy(fb[:, 0:T], ps_big[:, 0:T])
                    vdone(v)
            else:
                @blk.vector
                def _(v):
                    v.wait_ge(m_t, C["t"])
                    v.wait_ge(m_s2, C["s2"])
                    v.tensor_copy(fc[0:32, 0:T], ps_big[0:32, 0:T])
                    vdone(v)
        @blk.vector
        def _(v):
            v.tensor_copy(slab[:, 2 * T : 2 * T + T], fb[:, 0:T])
            v.tensor_copy(slab[0:32, o_bc : o_bc + T], fc[0:32, 0:T])
            vdone(v)
        @blk.sync
        def _(sy):
            sy.wait_ge(m_v, C["v"])
            i = sy.dma_start(out=bc_hbm[:], in_=slab[0:32, o_bc : o_bc + T])
            inc(i, "st", 16)

        o_xpb = 2 * T
        s_snap = {}
        v_snap = {}
        for dt in range(nD):
            o_xt = (dt % 2) * T
            @blk.sync
            def _(sy, dt=dt, o_xt=o_xt):
                if dt < 2:
                    sy.wait_ge(m_t, C["t"])
                if dt >= 2:
                    sy.wait_ge(m_v, v_snap[dt - 2])
                i = sy.dma_start(
                    out=slab[:, o_xt : o_xt + T],
                    in_=bass.AP(xT_hbm.tensor, dt * 128 * T, [[T, 128], [1, T]]))
                inc(i, "l%d" % (dt % 2), 16)
            @blk.tensor
            def _(t, dt=dt):
                if dt == 0:
                    t.wait_ge(m_v, C["v"])
                if dt >= 1:
                    t.wait_ge(m_s, s_snap[dt - 1])
                for tc in range(T // 512):
                    i = t.matmul(ps_big[:, tc * 512 : (tc + 1) * 512],
                                 slab[:, o_dtwT + dt * 128 : o_dtwT + (dt + 1) * 128],
                                 slab[:, o_xpb + tc * 512 : o_xpb + (tc + 1) * 512],
                                 start=True, stop=True)
                inc(i, "t")
            @blk.scalar
            def _(s, dt=dt):
                s.wait_ge(m_t, C["t"])
                if dt >= 1:
                    s.wait_ge(m_st, C["st"])
                    s.wait_ge(m_v, C["v"])
                if dt >= 1:
                    s.wait_ge(m_s, C["s"])
                sfin(s.activation(fa[:, 0:T], ps_big[:, 0:T], AF.Exp,
                                  bias=dtb_sb[:, dt : dt + 1]))
                s.wait_ge(m_s, C["s"])
                sfin(s.activation(slab[:, 2 * T + T : 2 * T + T + T], fa[:, 0:T],
                                  AF.Ln, bias=1.0))
                s_snap[dt] = C["s"]
            @blk.vector
            def _(v, dt=dt, o_xt=o_xt):
                v.wait_ge(m_l[dt % 2], C["l%d" % (dt % 2)])
                v.wait_ge(m_s, C["s"])
                if dt >= 1:
                    v.wait_ge(m_st, C["st"])
                o_dtt = 2 * T + T
                o_ut = 2 * T + 2 * T
                vfin(v.tensor_tensor(slab[:, o_ut : o_ut + T], slab[:, o_dtt : o_dtt + T],
                                     slab[:, o_xt : o_xt + T], OP.mult))
                vfin(v.tensor_scalar(slab[:, o_ut + T : o_ut + T + L],
                                     slab[:, o_xt + P : o_xt + T],
                                     Dv_sb[:, dt : dt + 1], None, OP.mult))
                v_snap[dt] = C["v"]
            @blk.sync
            def _(sy, dt=dt):
                sy.wait_ge(m_v, C["v"])
                o_dtt = 2 * T + T
                o_ut = 2 * T + 2 * T
                i = sy.dma_start(out=bass.AP(dt_hbm.tensor, dt * 128 * T, [[T, 128], [1, T]]),
                                 in_=slab[:, o_dtt : o_dtt + T])
                inc(i, "st", 16)
                i = sy.dma_start(out=bass.AP(u_hbm.tensor, dt * 128 * T, [[T, 128], [1, T]]),
                                 in_=slab[:, o_ut : o_ut + T])
                inc(i, "st", 16)
                i = sy.dma_start(out=bass.AP(gd_hbm.tensor, dt * 128 * L, [[L, 128], [1, L]]),
                                 in_=slab[:, o_ut + T : o_ut + T + L])
                inc(i, "st", 16)

        NQ = 4
        o_br = 0
        o_cr = o_br + N * P
        o_dA = o_cr + N * L
        o_dbx = o_dA + 8 * P
        o_h = o_dbx + 2 * P
        o_g = o_h + 2 * P
        o_dtu = o_g + 4 * L
        o_gd = o_dtu + 4 * P
        o_ones = o_gd + L
        assert o_ones + P <= SLAB

        ps_v_snap = {}
        ps_s_snap = {}
        ps_evac_snap = {}
        @blk.vector
        def _(v):
            v.wait_ge(m_st, C["st"])
            v.wait_ge(m_st2, C["st2"])
            vfin(v.memset(slab[:, o_ones : o_ones + P], 1.0))
        ps_v0 = C["v"]

        for ch in range(2):
            t0 = ch * P
            @blk.sync
            def _(sy, ch=ch, t0=t0):
                sy.wait_ge(m_v, C["v"])
                sy.wait_ge(m_v2, C["v2"])
                sy.wait_ge(m_st, C["st"])
                sy.wait_ge(m_st2, C["st2"])
                if DBG:
                    sy.wait_ge(m_o, C["o"])
                for n in range(N):
                    i = sy.dma_start(
                        out=slab[:, o_br + n * P : o_br + (n + 1) * P],
                        in_=bass.AP(bc_hbm.tensor, n * T + t0, [[0, 128], [1, P]]))
                    inc(i, "ld", 16)
                if ch == 1:
                    for n in range(N):
                        i = sy.dma_start(
                            out=slab[:, o_cr + n * L : o_cr + (n + 1) * L],
                            in_=bass.AP(bc_hbm.tensor, (N + n) * T + P, [[0, 128], [1, L]]))
                        inc(i, "ld", 16)

            for dt in range(nD):
                pyt = py0
                o_dt_ = o_dtu + (dt % 2) * P
                o_ut_ = o_dtu + 2 * P + (dt % 2) * P
                @blk.sync
                def _(sy, ch=ch, dt=dt, t0=t0, o_dt_=o_dt_, o_ut_=o_ut_):
                    key = (ch, dt - 2)
                    if key in ps_s_snap:
                        sy.wait_ge(m_s, ps_s_snap[key])
                        sy.wait_ge(m_v, ps_v_snap[key])
                    elif ch == 1 and dt < 2:
                        sy.wait_ge(m_s, ps_s_snap[(0, nD - 2 + dt)])
                        sy.wait_ge(m_v, ps_v_snap[(0, nD - 2 + dt)])
                    i = sy.dma_start(out=slab[:, o_dt_ : o_dt_ + P],
                                     in_=bass.AP(dt_hbm.tensor, dt * 128 * T + t0,
                                                 [[T, 128], [1, P]]))
                    inc(i, "l%d" % (dt % 2), 16)
                    i = sy.dma_start(out=slab[:, o_ut_ : o_ut_ + P],
                                     in_=bass.AP(u_hbm.tensor, dt * 128 * T + t0,
                                                 [[T, 128], [1, P]]))
                    inc(i, "l%d" % (dt % 2), 16)
                    if ch == 1:
                        if dt >= 1:
                            sy.wait_ge(m_v, ps_evac_snap[dt - 1])
                            sy.wait_ge(m_st, C["st"])
                        i = sy.dma_start(out=slab[:, o_gd : o_gd + L],
                                         in_=bass.AP(gd_hbm.tensor, dt * 128 * L,
                                                     [[L, 128], [1, L]]))
                        inc(i, "l%d" % (dt % 2), 16)

                if ch == 0:
                    @blk.vector
                    def _(v, dt=dt, o_dt_=o_dt_):
                        v.wait_ge(m_l[dt % 2], C["l%d" % (dt % 2)])
                        if dt >= 1:
                            v.wait_ge(m_s, C["s"])
                        vfin(nc.vector.tensor_tensor_scan(
                            fc[:, 0:P], slab[:, o_ones : o_ones + P],
                            slab[:, o_dt_ : o_dt_ + P], 0.0, OP.mult, OP.subtract))
                        v.wait_ge(m_v, C["v"])
                        vfin(v.tensor_scalar(bias16[:], A_sb[:, dt * N : (dt + 1) * N],
                                             fc[:, P - 1 : P], -1.0, OP.mult, OP.mult))
                else:
                    @blk.tensor
                    def _(t, dt=dt, pyt=pyt):
                        t.wait_ge(m_l[dt % 2], C["l%d" % (dt % 2)])
                        if dt >= 1:
                            t.wait_ge(m_v, ps_evac_snap[dt - 1])
                        for q in range(L // QL):
                            i = t.matmul(pyt[:, q * QL : (q + 1) * QL], ident[:],
                                         slab[:, o_gd + q * QL : o_gd + (q + 1) * QL],
                                         start=True, stop=False)
                        inc(i, "t")

                for q0 in range(0, N, NQ):
                    qb = o_dA + ((q0 // NQ) % 2) * NQ * P
                    if ch == 0:
                        @blk.scalar
                        def _(s, dt=dt, q0=q0, qb=qb):
                            if q0 == 0:
                                s.wait_ge(m_v, C["v"])
                            par = (q0 // NQ) % 2
                            for k in range(NQ):
                                n = q0 + k
                                wv = C.get(("dA", par, k), ps_v0)
                                s.wait_ge(m_v, wv)
                                sfin(s.activation(slab[:, qb + k * P : qb + (k + 1) * P],
                                                  fc[:, 0:P], AF.Exp,
                                                  scale=A_sb[:, dt * N + n : dt * N + n + 1],
                                                  bias=bias16[:, n : n + 1]))
                        @blk.vector
                        def _(v, dt=dt, q0=q0, qb=qb, o_ut_=o_ut_):
                            s_last = C["s"]
                            if q0 == 0:
                                v.wait_ge(m_l[dt % 2], C["l%d" % (dt % 2)])
                                v.wait_ge(m_ld, C["ld"])
                            ack = {}
                            def emit(tag, wait_tags, fn):
                                for w in wait_tags:
                                    wv = ack.get(w) if isinstance(w, str) else w
                                    if wv:
                                        v.wait_ge(m_v, wv)
                                i = fn()
                                i.then_inc(m_v, 1)
                                C["v"] += 1
                                ack[tag] = C["v"]
                            def B(k):
                                n = q0 + k
                                pb_ = o_dbx + (k % 2) * P
                                return v.tensor_tensor(slab[:, pb_ : pb_ + P],
                                                       slab[:, o_ut_ : o_ut_ + P],
                                                       slab[:, o_br + n * P : o_br + (n + 1) * P],
                                                       OP.mult)
                            def ST(k):
                                n = q0 + k
                                pb_ = o_dbx + (k % 2) * P
                                ph_ = o_h + (k % 2) * P
                                return nc.vector.scalar_tensor_tensor(
                                    slab[:, ph_ : ph_ + P], slab[:, pb_ : pb_ + P],
                                    1.0, slab[:, qb + k * P : qb + (k + 1) * P],
                                    OP.mult, OP.mult,
                                    accum_out=carry[:, dt * N + n : dt * N + n + 1])
                            def STW(k):
                                v.wait_ge(m_s, s_last - (NQ - 1 - k))
                                return ST(k)
                            emit("B0", [C.get("pbx0", 0)], lambda: B(0))
                            emit("B1", [C.get("pbx1", 0)], lambda: B(1))
                            emit("S0", ["B0"], lambda: STW(0))
                            emit("S1", ["B1"], lambda: STW(1))
                            emit("B2", ["S0"], lambda: B(2))
                            emit("B3", ["S1"], lambda: B(3))
                            emit("S2", ["B2", "S0"], lambda: STW(2))
                            emit("S3", ["B3", "S1"], lambda: STW(3))
                            C["pbx0"] = ack["S2"]
                            C["pbx1"] = ack["S3"]
                            par = (q0 // NQ) % 2
                            for k in range(NQ):
                                C[("dA", par, k)] = ack["S%d" % k]
                    else:
                        @blk.scalar
                        def _(s, dt=dt, q0=q0, qb=qb, o_dt_=o_dt_):
                            if q0 == 0:
                                s.wait_ge(m_l[dt % 2], C["l%d" % (dt % 2)])
                                s.wait_ge(m_ld, C["ld"])
                            par = (q0 // NQ) % 2
                            for k in range(NQ):
                                n = q0 + k
                                wv = C.get(("dA", par, k), ps_v0)
                                s.wait_ge(m_v, wv)
                                sfin(s.activation(slab[:, qb + k * P : qb + (k + 1) * P],
                                                  slab[:, o_dt_ : o_dt_ + P], AF.Exp,
                                                  scale=A_sb[:, dt * N + n : dt * N + n + 1]))
                        @blk.vector
                        def _(v, dt=dt, q0=q0, qb=qb, o_ut_=o_ut_):
                            s_last = C["s"]
                            if q0 == 0:
                                v.wait_ge(m_l[dt % 2], C["l%d" % (dt % 2)])
                                v.wait_ge(m_ld, C["ld"])
                            ack = {}
                            def emit(tag, wait_tags, fn):
                                for w in wait_tags:
                                    wv = ack.get(w) if isinstance(w, str) else w
                                    if wv:
                                        v.wait_ge(m_v, wv)
                                i = fn()
                                i.then_inc(m_v, 1)
                                C["v"] += 1
                                ack[tag] = C["v"]
                            def B(k):
                                n = q0 + k
                                pb_ = o_dbx + (k % 2) * P
                                return v.tensor_tensor(slab[:, pb_ : pb_ + P],
                                                       slab[:, o_ut_ : o_ut_ + P],
                                                       slab[:, o_br + n * P : o_br + (n + 1) * P],
                                                       OP.mult)
                            def TT_(k):
                                n = q0 + k
                                pb_ = o_dbx + (k % 2) * P
                                ph_ = o_h + (k % 2) * P
                                return nc.vector.tensor_tensor_scan(
                                    slab[:, ph_ : ph_ + P], slab[:, qb + k * P : qb + (k + 1) * P],
                                    slab[:, pb_ : pb_ + P],
                                    carry[:, dt * N + n : dt * N + n + 1],
                                    OP.mult, OP.add)
                            def CM(k):
                                n = q0 + k
                                ph_ = o_h + (k % 2) * P
                                pg = o_g + k * L
                                return v.tensor_tensor(slab[:, pg : pg + L],
                                                       slab[:, ph_ : ph_ + P],
                                                       slab[:, o_cr + n * L : o_cr + (n + 1) * L],
                                                       OP.mult)
                            def TW(k):
                                v.wait_ge(m_s, s_last - (NQ - 1 - k))
                                return TT_(k)
                            def CW(k):
                                wv = C.get(("gt", k), 0)
                                if wv:
                                    v.wait_ge(m_t, wv)
                                return CM(k)
                            emit("B0", [C.get("pbx0", 0)], lambda: B(0))
                            emit("B1", [C.get("pbx1", 0)], lambda: B(1))
                            emit("T0", ["B0"], lambda: TW(0))
                            emit("T1", ["B1"], lambda: TW(1))
                            emit("C0", ["T0"], lambda: CW(0))
                            emit("B2", ["T0"], lambda: B(2))
                            emit("C1", ["T1"], lambda: CW(1))
                            emit("B3", ["T1"], lambda: B(3))
                            emit("T2", ["B2", "C0"], lambda: TW(2))
                            emit("T3", ["B3", "C1"], lambda: TW(3))
                            emit("C2", ["T2"], lambda: CW(2))
                            emit("C3", ["T3"], lambda: CW(3))
                            C["pbx0"] = ack["T2"]
                            C["pbx1"] = ack["T3"]
                            par = (q0 // NQ) % 2
                            for k in range(NQ):
                                C[("dA", par, k)] = ack["T%d" % k]
                                C[("g", k)] = ack["C%d" % k]
                        @blk.tensor
                        def _(t, dt=dt, q0=q0):
                            for k in range(NQ):
                                n = q0 + k
                                pg = o_g + k * L
                                t.wait_ge(m_v, C[("g", k)])
                                for qq in range(L // QL):
                                    i = t.matmul(py0[:, qq * QL : (qq + 1) * QL], ident[:],
                                                 slab[:, pg + qq * QL : pg + (qq + 1) * QL],
                                                 start=False, stop=(n == N - 1))
                                inc(i, "t")
                                C[("gt", k)] = C["t"]
                ps_s_snap[(ch, dt)] = C["s"]
                ps_v_snap[(ch, dt)] = C["v"]
                if ch == 1:
                    @blk.vector
                    def _(v, dt=dt, pyt=pyt):
                        v.wait_ge(m_t, C["t"])
                        vfin(v.tensor_copy(slab[:, o_gd : o_gd + L], pyt[:, 0:L]))
                        ps_evac_snap[dt] = C["v"]
                    @blk.sync
                    def _(sy, dt=dt):
                        sy.wait_ge(m_v, C["v"])
                        i = sy.dma_start(
                            out=bass.AP(y_hbm.tensor, dt * 128 * L, [[L, 128], [1, L]]),
                            in_=slab[:, o_gd : o_gd + L])
                        inc(i, "st", 16)

        @blk.sync
        def _(sy):
            sy.wait_ge(m_st, C["st"])
            sy.wait_ge(m_st2, C["st2"])
            i = sy.dma_start(out=wsrep[:], in_=bass.AP(wsc_hbm.tensor, 0, [[0, 128], [1, D]]))
            inc(i, "ld", 16)

        o_ysl = 0
        GT0 = min(8, nD)
        o_xq = 2 * GT0 * 128
        o_xqT = o_xq + D
        o_wg = o_xqT + nD * 128 * TTH
        wg_t_snap = {}
        assert o_wg + 2 * nD * JG <= SLAB
        ps_ffn = [ps_big[:, 0:512], ps_big[:, 512:1024],
                  ps_big[:, 1024:1536], ps_big[:, 1536:2048]]

        for half in range(nHALF):
            tts = list(range(half * TTH, (half + 1) * TTH))
            yt_snap = {}
            yld_snap = {}
            for tt in tts:
                @blk.sync
                def _(sy, tt=tt):
                    sy.wait_ge(m_v, C["v"])
                    sy.wait_ge(m_s, C["s"])
                    i = sy.dma_start(out=fa[:], in_=x_all[P + tt * 128 : P + (tt + 1) * 128, :])
                    inc(i, "ld", 16)
                GT = min(8, nD)
                nG = nD // GT
                for g in range(nG):
                    ob = o_ysl + (g % 2) * GT * 128
                    @blk.sync
                    def _(sy, tt=tt, g=g, GT=GT, ob=ob):
                        if (tt, g - 2) in yt_snap:
                            sy.wait_ge(m_t, yt_snap[(tt, g - 2)])
                        dt0 = g * GT
                        i = sy.dma_start(
                            out=bass.AP(slab, ob, [[SLAB, 128], [128, GT], [1, 128]]),
                            in_=bass.AP(y_hbm.tensor, dt0 * 128 * L + tt * 128,
                                        [[L, 128], [128 * L, GT], [1, 128]]))
                        inc(i, "l%d" % (g % 2), 16)
                        yld_snap[(tt, g)] = C["l%d" % (g % 2)]
                    @blk.tensor
                    def _(t, tt=tt, g=g, GT=GT, ob=ob):
                        t.wait_ge(m_l[g % 2], yld_snap[(tt, g)])
                        if g > 0:
                            t.wait_ge(m_v, C["v"])
                        for k in range(GT):
                            i = t.matmul(ps_tr[:, k * 128 : (k + 1) * 128],
                                         slab[:, ob + k * 128 : ob + (k + 1) * 128],
                                         ident[:], is_transpose=True, start=True, stop=True)
                        inc(i, "t")
                        yt_snap[(tt, g)] = C["t"]
                    @blk.vector
                    def _(v, tt=tt, g=g, GT=GT):
                        v.wait_ge(m_t, C["t"])
                        if g == 0:
                            v.wait_ge(m_ld, C["ld"])
                            v.wait_ge(m_st, C["st"])
                            v.wait_ge(m_o, C["o"])
                        vfin(v.tensor_tensor(fb[:, g * GT * 128 : (g + 1) * GT * 128],
                                             ps_tr[:, 0 : GT * 128],
                                             fa[:, g * GT * 128 : (g + 1) * GT * 128],
                                             OP.add))
                @blk.vector
                def _(v, tt=tt):
                    v.wait_ge(m_v, C["v"])
                    vfin(v.reduce_sum(st[:, 4:5], fb[:], axis=AX.X))
                    v.wait_ge(m_v, C["v"])
                    vfin(v.tensor_scalar_mul(st[:, 5:6], st[:, 4:5], 1.0 / D))
                @blk.scalar
                def _(s, tt=tt):
                    s.wait_ge(m_v, C["v"])
                    sfin(s.activation(fc[:], fb[:], AF.Square, accum_out=st[:, 6:7]))
                @blk.vector
                def _(v, tt=tt):
                    v.wait_ge(m_s, C["s"])
                    vfin(v.tensor_scalar_mul(st[:, 7:8], st[:, 6:7], 1.0 / D))
                    vfin(v.tensor_tensor(st[:, 8:9], st[:, 5:6], st[:, 5:6], OP.mult))
                    v.wait_ge(m_v, C["v"])
                    vfin(v.tensor_tensor(st[:, 9:10], st[:, 7:8], st[:, 8:9], OP.subtract))
                    v.wait_ge(m_v, C["v"])
                    vfin(v.tensor_scalar_add(st[:, 9:10], st[:, 9:10], 1e-5))
                @blk.scalar
                def _(s, tt=tt):
                    s.wait_ge(m_v, C["v"])
                    sfin(s.activation(st[:, 10:11], st[:, 9:10], AF.Ln))
                    s.wait_ge(m_s, C["s"])
                    sfin(s.activation(st[:, 11:12], st[:, 10:11], AF.Exp, scale=-0.5))
                @blk.vector
                def _(v, tt=tt):
                    v.wait_ge(m_s, C["s"])
                    vfin(v.tensor_scalar(fb[:], fb[:], st[:, 5:6], st[:, 11:12],
                                         OP.subtract, OP.mult))
                @blk.sync
                def _(sy, tt=tt):
                    sy.wait_ge(m_v, C["v"])
                    i = sy.dma_start(out=bass.AP(xm_hbm.tensor, tt * 128 * D, [[D, 128], [1, D]]),
                                     in_=fb[:])
                    inc(i, "st", 16)
                @blk.vector
                def _(v, tt=tt):
                    v.wait_ge(m_v, C["v"])
                    vfin(v.reduce_max(st[:, 14:15], fb[:], axis=AX.X,
                                      apply_absolute_value=True))
                    v.wait_ge(m_v, C["v"])
                    vfin(v.reciprocal(st[:, 15:16], st[:, 14:15]))
                    v.wait_ge(m_v, C["v"])
                    vfin(v.tensor_scalar_mul(qsc[:, tt : tt + 1], st[:, 15:16], 127.0))
                    vfin(v.tensor_scalar_mul(asc[:, tt : tt + 1], st[:, 14:15], 1.0 / 127.0))
                @blk.scalar
                def _(s, tt=tt):
                    s.wait_ge(m_v, C["v"])
                    sfin(s.activation(fc[:], fb[:], AF.Copy, scale=qsc[:, tt : tt + 1]))
                @blk.vector
                def _(v, tt=tt):
                    v.wait_ge(m_s, C["s"])
                    vfin(v.tensor_copy(iq[:], fc[:, 0 : D // 2]))
                    v.wait_ge(m_v, C["v"])
                    vfin(v.tensor_copy(slab[:, o_xq : o_xq + D // 2], iq[:]))
                    v.wait_ge(m_v, C["v"])
                    vfin(v.tensor_copy(iq[:], fc[:, D // 2 : D]))
                    v.wait_ge(m_v, C["v"])
                    vfin(v.tensor_copy(slab[:, o_xq + D // 2 : o_xq + D], iq[:]))
                for g in range(nD // GT):
                    @blk.tensor
                    def _(t, tt=tt, g=g, GT=GT):
                        t.wait_ge(m_v, C["v"])
                        for k in range(GT):
                            dt = g * GT + k
                            i = t.matmul(ps_tr[:, k * 128 : (k + 1) * 128],
                                         slab[:, o_xq + dt * 128 : o_xq + (dt + 1) * 128],
                                         ident[:], is_transpose=True, start=True, stop=True)
                        inc(i, "t")
                    @blk.vector
                    def _(v, tt=tt, g=g, GT=GT):
                        v.wait_ge(m_t, C["t"])
                        toff = tt - half * TTH
                        dst = bass.AP(slab,
                                      o_xqT + (GT * g) * (128 * TTH) + toff * 128,
                                      [[SLAB, 128], [128 * TTH, GT], [1, 128]])
                        vfin(v.tensor_copy(dst, ps_tr[:, 0 : GT * 128]))
            for jg in range(nJG):
                o_wgb = o_wg + (jg % 2) * nD * JG
                @blk.sync
                def _(sy, jg=jg, o_wgb=o_wgb):
                    if jg >= 2:
                        sy.wait_ge(m_t, wg_t_snap[jg - 2])
                    jt0 = jg * JT_PER_G
                    DCH = min(8, nD)
                    for d0 in range(0, nD, DCH):
                        i = sy.dma_start(
                            out=bass.AP(slab, o_wgb + d0 * JG,
                                        [[SLAB, 128], [JG, DCH], [1, JG]]),
                            in_=bass.AP(wq_hbm.tensor,
                                        d0 * 128 * nJ * 128 + jt0 * 128,
                                        [[nJ * 128, 128], [128 * nJ * 128, DCH],
                                         [1, JG]]))
                        inc(i, "l%d" % (jg % 2), 16)
                for tt in tts:
                    toff = tt - half * TTH
                    @blk.tensor
                    def _(t, jg=jg, tt=tt, toff=toff, o_wgb=o_wgb):
                        t.wait_ge(m_l[jg % 2], C["l%d" % (jg % 2)])
                        t.wait_ge(m_v, C["v"])
                        for dt in range(nD):
                            i = t.matmul(ps_ffn[toff],
                                         slab[:, o_xqT + dt * (128 * TTH) + toff * 128
                                              : o_xqT + dt * (128 * TTH) + (toff + 1) * 128],
                                         slab[:, o_wgb + dt * JG : o_wgb + (dt + 1) * JG],
                                         start=(dt == 0), stop=(dt == nD - 1))
                        inc(i, "t")
                wg_t_snap[jg] = C["t"]
                GB = max(1, min(len(tts), D // 512))
                for b0 in range(0, len(tts), GB):
                    bts = list(range(b0, min(b0 + GB, len(tts))))
                    @blk.vector
                    def _(v, jg=jg, bts=bts):
                        v.wait_ge(m_t, C["t"])
                        v.wait_ge(m_s, C["s"])
                        for j, toff in enumerate(bts):
                            i = v.tensor_tensor(fc[:, j * 512 : (j + 1) * 512], ps_ffn[toff],
                                                wsrep[:, jg * 512 : (jg + 1) * 512], OP.mult)
                        vfin(i)
                    @blk.scalar
                    def _(s, jg=jg, bts=bts):
                        s.wait_ge(m_v, C["v"])
                        s.wait_ge(m_st, C["st"])
                        for j, toff in enumerate(bts):
                            i = s.activation(slab16[:, j * 512 : (j + 1) * 512],
                                             fc[:, j * 512 : (j + 1) * 512], AF.Copy,
                                             scale=asc[:, tts[toff] : tts[toff] + 1])
                        sfin(i)
                    @blk.sync
                    def _(sy, jg=jg, bts=bts):
                        sy.wait_ge(m_s, C["s"])
                        for j, toff in enumerate(bts):
                            i = sy.dma_start(
                                out=bass.AP(f16_hbm.tensor, tts[toff] * 128 * D + jg * 512,
                                            [[D, 128], [1, 512]]),
                                in_=slab16[:, j * 512 : (j + 1) * 512])
                            inc(i, "st", 16)
            sy_snap = {}
            for tt in tts:
                @blk.sync
                def _(sy, tt=tt):
                    sy.wait_ge(m_v, C["v"])
                    sy.wait_ge(m_st, C["st"])
                    i = sy.dma_start(out=fa[:], in_=bass.AP(xm_hbm.tensor, tt * 128 * D,
                                                            [[D, 128], [1, D]]))
                    inc(i, "ld", 16)
                    seg_snap = {}
                    for seg in range(D // 512):
                        fh = f16a if seg % 2 == 0 else f16b
                        if seg >= 2:
                            sy.wait_ge(m_v, C["v"] + seg - 1)
                        i = sy.dma_start(out=fh[:], in_=bass.AP(f16_hbm.tensor,
                                                                tt * 128 * D + seg * 512,
                                                                [[D, 128], [1, 512]]))
                        inc(i, "l%d" % (seg % 2), 16)
                        seg_snap[seg] = C["l%d" % (seg % 2)]
                    sy_snap[tt] = seg_snap
                @blk.vector
                def _(v, tt=tt):
                    v.wait_ge(m_ld, C["ld"])
                    v.wait_ge(m_o, C["o"])
                    for seg in range(D // 512):
                        fh = f16a if seg % 2 == 0 else f16b
                        v.wait_ge(m_l[seg % 2], sy_snap[tt][seg])
                        i = v.tensor_tensor(fb[:, seg * 512 : (seg + 1) * 512],
                                            fa[:, seg * 512 : (seg + 1) * 512], fh[:], OP.add)
                        i.then_inc(m_v, 1)
                        C["v"] += 1
                    v.wait_ge(m_v, C["v"])
                    vfin(v.reduce_sum(st[:, 4:5], fb[:], axis=AX.X))
                    v.wait_ge(m_v, C["v"])
                    vfin(v.tensor_scalar_mul(st[:, 5:6], st[:, 4:5], 1.0 / D))
                @blk.scalar
                def _(s, tt=tt):
                    s.wait_ge(m_v, C["v"])
                    sfin(s.activation(fc[:], fb[:], AF.Square, accum_out=st[:, 6:7]))
                @blk.vector
                def _(v, tt=tt):
                    v.wait_ge(m_s, C["s"])
                    vfin(v.tensor_scalar_mul(st[:, 7:8], st[:, 6:7], 1.0 / D))
                    vfin(v.tensor_tensor(st[:, 8:9], st[:, 5:6], st[:, 5:6], OP.mult))
                    v.wait_ge(m_v, C["v"])
                    vfin(v.tensor_tensor(st[:, 9:10], st[:, 7:8], st[:, 8:9], OP.subtract))
                    v.wait_ge(m_v, C["v"])
                    vfin(v.tensor_scalar_add(st[:, 9:10], st[:, 9:10], 1e-5))
                @blk.scalar
                def _(s, tt=tt):
                    s.wait_ge(m_v, C["v"])
                    sfin(s.activation(st[:, 10:11], st[:, 9:10], AF.Ln))
                    s.wait_ge(m_s, C["s"])
                    sfin(s.activation(st[:, 11:12], st[:, 10:11], AF.Exp, scale=-0.5))
                @blk.vector
                def _(v, tt=tt):
                    v.wait_ge(m_s, C["s"])
                    vfin(v.tensor_scalar(fb[:], fb[:], st[:, 5:6], st[:, 11:12],
                                         OP.subtract, OP.mult))
                @blk.sync
                def _(sy, tt=tt):
                    sy.wait_ge(m_v, C["v"])
                    i = sy.dma_start(out=out_hbm[tt * 128 : (tt + 1) * 128, :], in_=fb[:])
                    inc(i, "o", 16)

        @blk.gpsimd
        def _(g):
            g.wait_ge(m_o, C["o"])

    return nc


_NC_CACHE = {}


def kernel(L0_attn_x, L0_attn_A_log, L0_attn_D, L0_attn_x_proj_w,
           L0_attn_dt_proj_w, L0_attn_dt_proj_b, L0_ffn_weight):
    import ml_dtypes
    x = np.ascontiguousarray(np.asarray(L0_attn_x, dtype=np.float32))
    B, Tfull, Dm = x.shape
    N = np.asarray(L0_attn_A_log).shape[1]
    R = np.asarray(L0_attn_dt_proj_w).shape[1]
    P = Tfull // 2
    L = Tfull // 2

    key = (Dm, P, L, N, R)
    if key not in _NC_CACHE:
        _NC_CACHE[key] = _build(Dm, P, L, N, R)
    nc = _NC_CACHE[key]

    A_log = np.ascontiguousarray(np.asarray(L0_attn_A_log, np.float32))
    Dv = np.ascontiguousarray(np.asarray(L0_attn_D, np.float32))
    xpw = np.ascontiguousarray(np.asarray(L0_attn_x_proj_w, np.float32))
    dtw = np.ascontiguousarray(np.asarray(L0_attn_dt_proj_w, np.float32))
    dtbv = np.ascontiguousarray(np.asarray(L0_attn_dt_proj_b, np.float32))
    Wm = np.ascontiguousarray(np.asarray(L0_ffn_weight, np.float32))
    ident = np.eye(128, dtype=np.float32).astype(ml_dtypes.bfloat16)

    in_maps = []
    for c in range(8):
        b, h = c // 2, c % 2
        if h == 0:
            x_pre = np.zeros((P, Dm), np.float32)
            x_own = x[b, 0:L]
        else:
            x_pre = x[b, 0:P]
            x_own = x[b, P : P + L]
        x_allc = np.ascontiguousarray(np.concatenate([x_pre, x_own], axis=0))
        in_maps.append({
            "x_all": x_allc, "A_log": A_log, "D_vec": Dv, "xpw": xpw,
            "dtw": dtw, "dtb": dtbv, "W": Wm, "ident": ident,
        })

    res = run_bass_kernel_spmd(nc, in_maps, core_ids=list(range(8)))
    out = np.empty((B, Tfull, Dm), np.float32)
    for c in range(8):
        b, h = c // 2, c % 2
        out[b, h * L : (h + 1) * L] = res.results[c]["out"]
    return out
